# version 01 (final)
# baseline (speedup 1.0000x reference)
import math
import numpy as np
import concourse.bass as bass
import concourse.mybir as mybir
from concourse.bass_utils import run_bass_kernel_spmd

F32 = mybir.dt.float32
BF16 = mybir.dt.bfloat16
U8 = mybir.dt.uint8
I32 = mybir.dt.int32
AF = mybir.ActivationFunctionType
ALU = mybir.AluOpType
AX = mybir.AxisListType

D = 1024
T = 2048
FC = 8
NCORE = 8
TB = 4
TBW = 512

ENGINES = ("tensor", "vector", "scalar", "gpsimd", "sync")


class Ctx:

    def __init__(self, nc):
        self.nc = nc
        self.streams = {e: [] for e in ENGINES}
        self.sems = {}
        self.cnt = {}
        self.sbuf = {}
        self.psum = {}
        self._uid = 0

    def uid(self):
        self._uid += 1
        return self._uid

    def sem(self, name):
        if name not in self.sems:
            self.sems[name] = self.nc.alloc_semaphore(name)
            self.cnt[name] = 0
        return self.sems[name]

    def inc(self, name, by):
        self.sem(name)
        self.cnt[name] += by
        return self.cnt[name]

    def val(self, name):
        self.sem(name)
        return self.cnt[name]

    def dram(self, name, shape, dtype=F32, kind="ExternalInput"):
        key = "dram_" + name
        if key not in self.sbuf:
            self.sbuf[key] = self.nc.dram_tensor(name, shape, dtype, kind=kind).ap()
        return self.sbuf[key]

    def sb(self, name, shape, dtype):
        if name not in self.sbuf:
            self.sbuf[name] = self.nc.alloc_sbuf_tensor(name, shape, dtype)
        return self.sbuf[name]

    def alias(self, name, shape, dtype, base_names):
        if name in self.sbuf:
            return self.sbuf[name]
        addr = self.nc.lookup_mloc(self.sbuf[base_names[0]]).addr
        end = self.nc.lookup_mloc(self.sbuf[base_names[-1]])
        need = int(np.prod(shape[1:])) * mybir.dt.size(dtype)
        avail = end.addr + end.size() - addr
        assert avail >= need, (name, avail, need)
        self.sbuf[name] = self.nc.alloc_sbuf_tensor_at(name, shape, dtype,
                                                       offset=addr)
        return self.sbuf[name]

    def ps(self, name, shape, dtype=F32):
        if name not in self.psum:
            self.psum[name] = self.nc.alloc_psum_tensor(name, shape, dtype)
        return self.psum[name]

    def emit(self, eng, fn):
        self.streams[eng].append(fn)

    def build(self):
        nc = self.nc
        with nc.Block() as block:
            @block.tensor
            def _(t):
                for fn in self.streams["tensor"]:
                    fn(t)

            @block.vector
            def _(v):
                for fn in self.streams["vector"]:
                    fn(v)

            @block.scalar
            def _(s):
                for fn in self.streams["scalar"]:
                    fn(s)

            @block.gpsimd
            def _(g):
                for fn in self.streams["gpsimd"]:
                    fn(g)

            @block.sync
            def _(s):
                for fn in self.streams["sync"]:
                    fn(s)
        return nc



def emit_dma(ctx, eng, dst, src, sem_name, note=""):
    v = ctx.inc(sem_name, 16)
    sem = ctx.sem(sem_name)
    def fn(e, dst=dst, src=src, sem=sem):
        e.dma_start(out=dst, in_=src).then_inc(sem, 16)
    ctx.emit(eng, fn)
    return v


def emit_dma_transpose(ctx, eng, dst, src, sem_name):
    v = ctx.inc(sem_name, 16)
    sem = ctx.sem(sem_name)
    def fn(e, dst=dst, src=src, sem=sem):
        e.dma_start(out=dst, in_=src, transpose=True).then_inc(sem, 16)
    ctx.emit(eng, fn)
    return v


def emit_wait(ctx, eng, sem_name, v):
    sem = ctx.sem(sem_name)
    ctx.emit(eng, lambda e, sem=sem, v=v: e.wait_ge(sem, v))


def load_const_tiles(ctx):
    nc = ctx.nc
    cst = nc.dram_tensor("consts", [128, 256], F32, kind="ExternalInput").ap()
    idb = ctx.sb("c_idb", [128, 128], BF16)
    ones = ctx.sb("c_ones", [128, 8], BF16)
    onesrow = ctx.sb("c_onesrow", [1, 128], BF16)
    onesrowf = ctx.sb("c_onesrowf", [1, 128], F32)
    idf = ctx.sb("c_idf", [128, 128], F32)
    gp = ctx.sem("gp")
    emit_dma(ctx, "gpsimd", idb[:, :], cst[:, 0:128], "ld")
    emit_dma(ctx, "gpsimd", idf[:, :], cst[:, 0:128], "ld")
    gpv = ctx.inc("gp", 1)
    def fill(g, ones=ones, onesrow=onesrow, onesrowf=onesrowf, gp=gp):
        g.memset(ones[:, :], 1.0)
        g.memset(onesrow[:, :], 1.0)
        g.memset(onesrowf[:, :], 1.0).then_inc(gp, 1)
    ctx.emit("gpsimd", fill)
    return idb, ones, onesrow, gpv


def host_consts():
    c = np.zeros((128, 256), np.float32)
    c[:, 0:128] = np.eye(128, dtype=np.float32)
    return c



def emit_layernorm(ctx, tag, x, y, gamma_col=None, beta_col=None,
                   y0=None, dep=None, rstd_keep=None, x_dram=None):
    if x_dram is not None:
        xt = ctx.sb("sc_xt", [128, FC * TBW], F32)
    xb = ctx.sb("sc_xb", [128, FC * TBW], BF16)
    sq = ctx.sb("sc_sq", [128, FC * TBW], BF16)
    rows = ctx.sb("sc_rows", [1, 4 * TBW], F32)
    mb = ctx.sb("sc_mb", [128, TBW], BF16)
    rb = ctx.sb("sc_rb", [128, TBW], BF16)
    pln0 = ctx.ps("pln0", [1, TBW])
    pln1 = ctx.ps("pln1", [1, TBW])
    pbca = ctx.ps("p512a", [128, TBW])
    pbcb = ctx.ps("p512b", [128, TBW])
    ones = ctx.sbuf["c_ones"]
    onesrow = ctx.sbuf["c_onesrow"]
    onesrowf = ctx.sbuf["c_onesrowf"]
    mm = ctx.sem("mm"); ev = ctx.sem("ev"); ac = ctx.sem("ac")

    if dep is not None:
        emit_wait(ctx, "vector", dep[0], dep[1])
        emit_wait(ctx, "scalar", dep[0], dep[1])
    for tb in range(TB):
        c0 = tb * TBW
        if tb > 0:
            emit_wait(ctx, "vector", "mm", ctx.val("mm"))
            emit_wait(ctx, "scalar", "mm", ctx.val("mm"))
        if x_dram is not None:
            if tb > 0:
                emit_wait(ctx, "gpsimd", "ev", ctx.val("ev"))
                emit_wait(ctx, "gpsimd", "ac", ctx.val("ac"))
            lx = emit_dma(ctx, "gpsimd",
                          xt.ap().rearrange("p (fc w) -> p fc w", w=TBW),
                          x_dram.rearrange("(fc p) t -> p fc t", p=128)[:, :, c0:c0 + TBW],
                          "ld")
            emit_wait(ctx, "vector", "ld", lx)
            emit_wait(ctx, "scalar", "ld", lx)
            xs, off = xt, 0
        else:
            xs, off = x, c0
        def cvt(v, xs=xs, xb=xb, off=off, full=(x_dram is None)):
            for fc in range(FC):
                w = T if full else TBW
                ins = v.tensor_copy(xb[:, fc * TBW:(fc + 1) * TBW],
                                    xs[:, fc * w + off:fc * w + off + TBW])
            ins.then_inc(ev, 1)
        vcvt = ctx.inc("ev", 1)
        ctx.emit("vector", cvt)
        def sqr(s, xs=xs, sq=sq, off=off, full=(x_dram is None)):
            for fc in range(FC):
                w = T if full else TBW
                ins = s.activation(out=sq[:, fc * TBW:(fc + 1) * TBW],
                                   in_=xs[:, fc * w + off:fc * w + off + TBW],
                                   func=AF.Square)
            ins.then_inc(ac, 1)
        asq = ctx.inc("ac", 1)
        ctx.emit("scalar", sqr)
        emit_wait(ctx, "tensor", "ev", vcvt)
        emit_wait(ctx, "tensor", "ac", asq)
        if tb > 0:
            emit_wait(ctx, "tensor", "ev", ctx.val("ev"))
        def sums(t, xb=xb, sq=sq, pln0=pln0, pln1=pln1, ones=ones):
            for fc in range(FC):
                t.matmul(pln0[0:1, :], ones[:, 0:1], xb[:, fc * TBW:(fc + 1) * TBW],
                         start=(fc == 0), stop=(fc == FC - 1))
            for fc in range(FC):
                ins = t.matmul(pln1[0:1, :], ones[:, 0:1], sq[:, fc * TBW:(fc + 1) * TBW],
                               start=(fc == 0), stop=(fc == FC - 1))
            ins.then_inc(mm, 1)
        msum = ctx.inc("mm", 1)
        ctx.emit("tensor", sums)
        emit_wait(ctx, "vector", "mm", msum)
        def stats(v, pln0=pln0, pln1=pln1, rows=rows):
            v.tensor_scalar_mul(rows[0:1, 0:TBW], pln0[0:1, :], 1.0 / 1024.0)
            v.tensor_scalar_mul(rows[0:1, TBW:2 * TBW], pln1[0:1, :], 1.0 / 1024.0)
            v.tensor_tensor(rows[0:1, 2 * TBW:3 * TBW],
                            rows[0:1, 0:TBW], rows[0:1, 0:TBW], ALU.mult)
            ins = v.scalar_tensor_tensor(rows[0:1, TBW:2 * TBW],
                                         rows[0:1, TBW:2 * TBW], 1e-5,
                                         rows[0:1, 2 * TBW:3 * TBW],
                                         op0=ALU.add, op1=ALU.subtract)
            ins.then_inc(ev, 1)
        vst = ctx.inc("ev", 1)
        ctx.emit("vector", stats)
        emit_wait(ctx, "scalar", "ev", vst)
        def rstd(s, rows=rows):
            s.activation(out=rows[0:1, 2 * TBW:3 * TBW], in_=rows[0:1, TBW:2 * TBW],
                         func=AF.Ln)
            ins = s.activation(out=rows[0:1, 3 * TBW:4 * TBW],
                               in_=rows[0:1, 2 * TBW:3 * TBW], func=AF.Exp, scale=-0.5)
            ins.then_inc(ac, 1)
        ars = ctx.inc("ac", 1)
        ctx.emit("scalar", rstd)
        if rstd_keep is not None:
            emit_wait(ctx, "vector", "ac", ars)
            def keep(v, rows=rows, rk=rstd_keep, c0=c0):
                ins = v.tensor_copy(rk[0:1, c0:c0 + TBW], rows[0:1, 3 * TBW:4 * TBW])
                ins.then_inc(ev, 1)
            ctx.inc("ev", 1)
            ctx.emit("vector", keep)
        emit_wait(ctx, "tensor", "ac", ars)
        def bcast(t, rows=rows, pbca=pbca, pbcb=pbcb, onesrowf=onesrowf):
            t.matmul(pbca[:, :], onesrowf[0:1, :], rows[0:1, 0:TBW],
                     start=True, stop=True)
            ins = t.matmul(pbcb[:, :], onesrowf[0:1, :], rows[0:1, 3 * TBW:4 * TBW],
                           start=True, stop=True)
            ins.then_inc(mm, 1)
        mbc = ctx.inc("mm", 1)
        ctx.emit("tensor", bcast)
        emit_wait(ctx, "vector", "mm", mbc)
        def cpbc(v, pbca=pbca, pbcb=pbcb, mb=mb, rb=rb):
            v.tensor_copy(mb[:, :], pbca[:, :])
            ins = v.tensor_copy(rb[:, :], pbcb[:, :])
            ins.then_inc(ev, 1)
        ctx.inc("ev", 1)
        ctx.emit("vector", cpbc)
        def apply(v, xb=xb, mb=mb, rb=rb, y=y, y0=y0, c0=c0,
                  gamma_col=gamma_col, beta_col=beta_col):
            for fc in range(FC):
                so = slice(fc * T + c0, fc * T + c0 + TBW)
                si = slice(fc * TBW, (fc + 1) * TBW)
                v.tensor_tensor(y[:, so], xb[:, si], mb[:, :], ALU.subtract)
                ins = v.tensor_tensor(y[:, so], y[:, so], rb[:, :], ALU.mult)
                if y0 is not None:
                    ins = v.tensor_copy(y0[:, so], y[:, so])
                if gamma_col is not None:
                    ins = v.tensor_scalar(y[:, so], y[:, so],
                                          gamma_col[:, fc:fc + 1],
                                          beta_col[:, fc:fc + 1],
                                          op0=ALU.mult, op1=ALU.add)
            ins.then_inc(ev, 1)
        ctx.inc("ev", 1)
        ctx.emit("vector", apply)
    return ctx.val("ev")



def emit_s1(ctx, xres, dep_ev):
    nc = ctx.nc
    H, DK = 16, 64
    wq_d = nc.dram_tensor("s1_wq", [D, D], F32, kind="ExternalInput").ap()
    wk_d = nc.dram_tensor("s1_wk", [D, DK], F32, kind="ExternalInput").ap()
    wv_d = nc.dram_tensor("s1_wv", [D, DK], F32, kind="ExternalInput").ap()
    wo_d = nc.dram_tensor("s1_wo", [D, D], F32, kind="ExternalInput").ap()
    bq_d = nc.dram_tensor("s1_bq", [128, 8], F32, kind="ExternalInput").ap()
    bk_d = nc.dram_tensor("s1_bk", [128, 1], F32, kind="ExternalInput").ap()
    bv_d = nc.dram_tensor("s1_bv", [128, 1], F32, kind="ExternalInput").ap()
    bo_d = nc.dram_tensor("s1_bo", [128, 8], F32, kind="ExternalInput").ap()

    wq = ctx.sb("sc_w", [128, FC * D], BF16)
    wk = ctx.sb("sc_wk", [128, FC * DK], BF16)
    wv = ctx.sb("sc_wv", [128, FC * DK], BF16)
    wo = wq
    bq = ctx.sb("s1_bqs", [128, 8], F32)
    bkv = ctx.sb("s1_bkvs", [128, 2], F32)
    bvb = ctx.sb("s1_bvb", [1, 64], BF16)
    bo = ctx.sb("s1_bos", [128, 8], F32)

    def wap(w_dram, gwidth):
        return w_dram.rearrange("(fc p) g -> p fc g", p=128)

    def w3(sb_t, gw):
        return sb_t.ap().rearrange("p (fc g) -> p fc g", g=gw)
    ld0 = emit_dma(ctx, "gpsimd", w3(wq, D), wap(wq_d, D), "wl")
    emit_dma(ctx, "gpsimd", w3(wk, DK), wap(wk_d, DK), "wl")
    emit_dma(ctx, "gpsimd", w3(wv, DK), wap(wv_d, DK), "wl")
    emit_dma(ctx, "gpsimd", bq[:, :], bq_d[:, :], "wl")
    emit_dma(ctx, "gpsimd", bkv[:, 0:1], bk_d[:, :], "wl")
    emit_dma(ctx, "gpsimd", bkv[:, 1:2], bv_d[:, :], "wl")
    emit_dma(ctx, "gpsimd", bo[:, :], bo_d[:, :], "wl")
    wl_all = emit_dma(ctx, "gpsimd", bvb[:, :], bv_d.rearrange("p one -> one p")[0:1, 0:64], "wl")

    y = ctx.sb("sc_big_y", [128, FC * T], BF16)
    ln_ev = emit_layernorm(ctx, "s1", xres, y, dep=("ev", dep_ev))

    mm = ctx.sem("mm"); ev = ctx.sem("ev"); ac = ctx.sem("ac")

    qsb = ctx.sb("sc_big_a", [128, FC * T], BF16)
    ksb = ctx.sb("sc_k", [128, T], BF16)
    vsb = ctx.sb("sc_v", [128, 16 * 65], BF16)
    qsq = ctx.sb("sc_sq", [128, FC * TBW], BF16)
    p512a = ctx.ps("p512a", [128, TBW])
    p512b = ctx.ps("p512b", [128, TBW])
    p512c = ctx.ps("p512c", [128, TBW])
    pav = ctx.ps("pav", [65, TBW])
    prow = ctx.ps("prow", [1, TBW])
    ones = ctx.sbuf["c_ones"]
    idb = ctx.sbuf["c_idb"]

    emit_wait(ctx, "tensor", "wl", wl_all)
    emit_wait(ctx, "tensor", "ev", ln_ev)

    for gc in range(FC):
        for tb in range(TB):
            pq = p512a if (gc * TB + tb) % 2 == 0 else p512b
            if gc * TB + tb >= 2:
                emit_wait(ctx, "tensor", "ev", ctx.val("ev"))
            def qmm(t, gc=gc, tb=tb, pq=pq, wq=wq, y=y):
                for fc in range(FC):
                    ins = t.matmul(pq[:, :], wq[:, fc * D + gc * 128:fc * D + gc * 128 + 128],
                                   y[:, fc * T + tb * TBW:fc * T + tb * TBW + TBW],
                                   start=(fc == 0), stop=(fc == FC - 1))
                ins.then_inc(mm, 1)
            mq = ctx.inc("mm", 1)
            ctx.emit("tensor", qmm)
            emit_wait(ctx, "vector", "mm", mq)
            def qev(v, gc=gc, tb=tb, pq=pq, qsb=qsb, bq=bq):
                ins = v.tensor_scalar_add(qsb[:, gc * T + tb * TBW:gc * T + tb * TBW + TBW],
                                          pq[:, :], bq[:, gc:gc + 1])
                ins.then_inc(ev, 1)
            ctx.inc("ev", 1)
            ctx.emit("vector", qev)
    for tb in range(TB):
        emit_wait(ctx, "tensor", "ev", ctx.val("ev"))
        def kmm(t, tb=tb, wk=wk, y=y, p=p512a):
            for fc in range(FC):
                ins = t.matmul(p[0:64, :], wk[:, fc * DK:fc * DK + DK],
                               y[:, fc * T + tb * TBW:fc * T + tb * TBW + TBW],
                               start=(fc == 0), stop=(fc == FC - 1))
            ins.then_inc(mm, 1)
        mk = ctx.inc("mm", 1)
        ctx.emit("tensor", kmm)
        emit_wait(ctx, "vector", "mm", mk)
        def kev(v, tb=tb, p=p512a, ksb=ksb, bkv=bkv):
            v.tensor_scalar_add(ksb[0:64, tb * TBW:tb * TBW + TBW], p[0:64, :],
                                bkv[0:64, 0:1])
            ins = v.tensor_copy(ksb[64:128, tb * TBW:tb * TBW + TBW],
                                ksb[0:64, tb * TBW:tb * TBW + TBW])
            ins.then_inc(ev, 1)
        ctx.inc("ev", 1)
        ctx.emit("vector", kev)
    for tc in range(16):
        emit_wait(ctx, "tensor", "ev", ctx.val("ev"))
        onesrow = ctx.sbuf["c_onesrow"]
        def vmm(t, tc=tc, wv=wv, y=y, p=p512b, onesrow=onesrow, bvb=bvb):
            for fc in range(FC):
                t.matmul(p[:, 0:64], y[:, fc * T + tc * 128:fc * T + tc * 128 + 128],
                         wv[:, fc * DK:fc * DK + DK],
                         start=(fc == 0), stop=False)
            ins = t.matmul(p[:, 0:64], onesrow[0:1, 0:128],
                           bvb[0:1, :], start=False, stop=True)
            ins.then_inc(mm, 1)
        mv = ctx.inc("mm", 1)
        ctx.emit("tensor", vmm)
        emit_wait(ctx, "vector", "mm", mv)
        def vev(v, tc=tc, p=p512b, vsb=vsb):
            v.tensor_copy(vsb[:, tc * 65:tc * 65 + 64], p[:, 0:64])
            ins = v.memset(vsb[:, tc * 65 + 64:tc * 65 + 65], 1.0)
            ins.then_inc(ev, 1)
        ctx.inc("ev", 1)
        ctx.emit("vector", vev)

    qk_ev = ctx.val("ev")
    emit_wait(ctx, "gpsimd", "ev", qk_ev)
    wl_wo = emit_dma(ctx, "gpsimd", w3(wo, D), wap(wo_d, D), "wl")
    rows2 = ctx.sb("sc_rows2", [1, T + 32], F32)
    for tb in range(TB):
        emit_wait(ctx, "scalar", "ev", qk_ev)
        if tb > 0:
            emit_wait(ctx, "scalar", "mm", ctx.val("mm"))
            emit_wait(ctx, "tensor", "ev", ctx.val("ev"))
        def sqq(s, tb=tb, qsb=qsb, qsq=qsq):
            for fc in range(FC):
                ins = s.activation(out=qsq[:, fc * TBW:(fc + 1) * TBW],
                                   in_=qsb[:, fc * T + tb * TBW:fc * T + tb * TBW + TBW],
                                   func=AF.Square)
            ins.then_inc(ac, 1)
        aqq = ctx.inc("ac", 1)
        ctx.emit("scalar", sqq)
        emit_wait(ctx, "tensor", "ac", aqq)
        def nq(t, qsq=qsq, ones=ones, prow=prow):
            for fc in range(FC):
                ins = t.matmul(prow[0:1, :], ones[:, 0:1],
                               qsq[:, fc * TBW:(fc + 1) * TBW],
                               start=(fc == 0), stop=(fc == FC - 1))
            ins.then_inc(mm, 1)
        mq = ctx.inc("mm", 1)
        ctx.emit("tensor", nq)
        emit_wait(ctx, "vector", "mm", mq)
        def nqe(v, tb=tb, prow=prow, rows2=rows2):
            ins = v.reduce_max(rows2[0:1, T + tb:T + tb + 1], prow[0:1, :], axis=AX.X)
            ins.then_inc(ev, 1)
        ctx.inc("ev", 1)
        ctx.emit("vector", nqe)
    for tb in range(TB):
        emit_wait(ctx, "scalar", "mm", ctx.val("mm"))
        emit_wait(ctx, "tensor", "ev", ctx.val("ev"))
        def skk(s, tb=tb, ksb=ksb, qsq=qsq):
            ins = s.activation(out=qsq[0:64, 0:TBW],
                               in_=ksb[0:64, tb * TBW:tb * TBW + TBW], func=AF.Square)
            ins.then_inc(ac, 1)
        akk = ctx.inc("ac", 1)
        ctx.emit("scalar", skk)
        emit_wait(ctx, "tensor", "ac", akk)
        def nk(t, qsq=qsq, ones=ones, prow=prow):
            ins = t.matmul(prow[0:1, :], ones[0:64, 0:1],
                           qsq[0:64, 0:TBW], start=True, stop=True)
            ins.then_inc(mm, 1)
        mk2 = ctx.inc("mm", 1)
        ctx.emit("tensor", nk)
        emit_wait(ctx, "vector", "mm", mk2)
        def nke(v, tb=tb, prow=prow, rows2=rows2):
            ins = v.reduce_max(rows2[0:1, T + 8 + tb:T + 9 + tb], prow[0:1, :], axis=AX.X)
            ins.then_inc(ev, 1)
        ctx.inc("ev", 1)
        ctx.emit("vector", nke)
    nrm_ev = ctx.val("ev")
    emit_wait(ctx, "vector", "ev", nrm_ev)
    def mx(v, rows2=rows2):
        v.reduce_max(rows2[0:1, T + 16:T + 17], rows2[0:1, T:T + 4], axis=AX.X)
        ins = v.reduce_max(rows2[0:1, T + 17:T + 18], rows2[0:1, T + 8:T + 12], axis=AX.X)
        ins.then_inc(ev, 1)
    vmx = ctx.inc("ev", 1)
    ctx.emit("vector", mx)
    emit_wait(ctx, "scalar", "ev", vmx)
    def csc(s, rows2=rows2):
        s.activation(out=rows2[0:1, T + 18:T + 19], in_=rows2[0:1, T + 16:T + 17],
                     func=AF.Ln)
        ins = s.activation(out=rows2[0:1, T + 19:T + 20], in_=rows2[0:1, T + 17:T + 18],
                           func=AF.Ln)
        ins.then_inc(ac, 1)
    acs = ctx.inc("ac", 1)
    ctx.emit("scalar", csc)
    emit_wait(ctx, "vector", "ac", acs)
    def csum(v, rows2=rows2):
        ins = v.tensor_add(rows2[0:1, T + 20:T + 21], rows2[0:1, T + 18:T + 19],
                           rows2[0:1, T + 19:T + 20])
        ins.then_inc(ev, 1)
    vcs = ctx.inc("ev", 1)
    ctx.emit("vector", csum)
    emit_wait(ctx, "scalar", "ev", vcs)
    def cexp(s, rows2=rows2):
        ins = s.activation(out=rows2[0:1, T + 21:T + 22], in_=rows2[0:1, T + 20:T + 21],
                           func=AF.Exp, scale=0.5)
        ins.then_inc(ac, 1)
    ace = ctx.inc("ac", 1)
    ctx.emit("scalar", cexp)
    emit_wait(ctx, "vector", "ac", ace)
    def cneg(v, rows2=rows2):
        ins = v.tensor_scalar_mul(rows2[0:1, T + 22:T + 23], rows2[0:1, T + 21:T + 22],
                                  -0.125)
        ins.then_inc(ev, 1)
    vcn = ctx.inc("ev", 1)
    ctx.emit("vector", cneg)
    emit_wait(ctx, "tensor", "ev", vcn)
    onesrowf = ctx.sbuf["c_onesrowf"]
    def cbc(t, rows2=rows2, p512c=p512c, onesrowf=onesrowf):
        ins = t.matmul(p512c[:, 0:1], onesrowf[0:1, :], rows2[0:1, T + 22:T + 23],
                       start=True, stop=True)
        ins.then_inc(mm, 1)
    mcb = ctx.inc("mm", 1)
    ctx.emit("tensor", cbc)
    bccol = ctx.sb("sc_bc", [128, 1], F32)
    emit_wait(ctx, "vector", "mm", mcb)
    def cbe(v, p512c=p512c, bccol=bccol):
        ins = v.tensor_copy(bccol[:, :], p512c[:, 0:1])
        ins.then_inc(ev, 1)
    vbe = ctx.inc("ev", 1)
    ctx.emit("vector", cbe)

    osb = ctx.sb("sc_big_y", [128, FC * T], BF16)
    onesrowf = ctx.sbuf["c_onesrowf"]
    ptile = ctx.sb("sc_pt", [128, 16 * TBW], BF16)
    binv = ctx.sb("sc_binv", [64, TBW], BF16)
    rrow = ctx.sb("sc_rrow", [1, TBW], F32)
    emit_wait(ctx, "tensor", "ev", vbe)
    emit_wait(ctx, "scalar", "ev", vbe)
    for h in range(16):
        hb = (h % 2) * 64
        gc = h // 2
        for qb in range(TB):
            q0 = qb * TBW
            for kt in range(16):
                pS = p512a if kt % 2 == 0 else p512b
                if h + qb + kt > 0:
                    emit_wait(ctx, "tensor", "ac", ctx.val("ac"))
                def smm(t, h=h, hb=hb, gc=gc, q0=q0, kt=kt, pS=pS, ksb=ksb, qsb=qsb):
                    ins = t.matmul(pS[:, :], ksb[hb:hb + 64, kt * 128:kt * 128 + 128],
                                   qsb[hb:hb + 64, gc * T + q0:gc * T + q0 + TBW],
                                   start=True, stop=True)
                    ins.then_inc(mm, 1)
                ms = ctx.inc("mm", 1)
                ctx.emit("tensor", smm)
                emit_wait(ctx, "scalar", "mm", ms)
                def sexp(s, kt=kt, pS=pS, ptile=ptile, bccol=bccol):
                    ins = s.activation(out=ptile[:, kt * TBW:kt * TBW + TBW], in_=pS[:, :],
                                       func=AF.Exp, scale=0.125, bias=bccol[:, 0:1])
                    ins.then_inc(ac, 1)
                ctx.inc("ac", 1)
                ctx.emit("scalar", sexp)
            emit_wait(ctx, "tensor", "ac", ctx.val("ac"))
            emit_wait(ctx, "tensor", "ev", ctx.val("ev"))
            def avm(t, pav=pav, vsb=vsb, ptile=ptile):
                for kt in range(16):
                    ins = t.matmul(pav[:, :], vsb[:, kt * 65:kt * 65 + 65],
                                   ptile[:, kt * TBW:kt * TBW + TBW],
                                   start=(kt == 0), stop=(kt == 15))
                ins.then_inc(mm, 1)
            mav = ctx.inc("mm", 1)
            ctx.emit("tensor", avm)
            emit_wait(ctx, "vector", "mm", mav)
            def rec(v, pav=pav, rrow=rrow):
                ins = v.reciprocal(rrow[0:1, :], pav[64:65, :])
                ins.then_inc(ev, 1)
            vrc = ctx.inc("ev", 1)
            ctx.emit("vector", rec)
            emit_wait(ctx, "tensor", "ev", vrc)
            def rbc(t, rrow=rrow, p512c=p512c, onesrowf=onesrowf):
                ins = t.matmul(p512c[0:64, :], onesrowf[0:1, 0:64], rrow[0:1, :],
                               start=True, stop=True)
                ins.then_inc(mm, 1)
            mrb = ctx.inc("mm", 1)
            ctx.emit("tensor", rbc)
            emit_wait(ctx, "vector", "mm", mrb)
            def div(v, h=h, hb=hb, gc=gc, q0=q0, pav=pav, p512c=p512c, binv=binv, osb=osb):
                v.tensor_copy(binv[:, :], p512c[0:64, :])
                ins = v.tensor_tensor(osb[hb:hb + 64, gc * T + q0:gc * T + q0 + TBW],
                                      pav[0:64, :], binv[:, :], ALU.mult)
                ins.then_inc(ev, 1)
            ctx.inc("ev", 1)
            ctx.emit("vector", div)
    att_ev = ctx.val("ev")
    emit_wait(ctx, "tensor", "ev", att_ev)
    emit_wait(ctx, "tensor", "wl", wl_wo)
    for dc in range(FC):
        for tb in range(TB):
            po = p512a if (dc * TB + tb) % 2 == 0 else p512b
            emit_wait(ctx, "tensor", "ev", ctx.val("ev"))
            def omm(t, dc=dc, tb=tb, po=po, wo=wo, osb=osb):
                for fc in range(FC):
                    ins = t.matmul(po[:, :], wo[:, fc * D + dc * 128:fc * D + dc * 128 + 128],
                                   osb[:, fc * T + tb * TBW:fc * T + tb * TBW + TBW],
                                   start=(fc == 0), stop=(fc == FC - 1))
                ins.then_inc(mm, 1)
            mo = ctx.inc("mm", 1)
            ctx.emit("tensor", omm)
            emit_wait(ctx, "vector", "mm", mo)
            def oev(v, dc=dc, tb=tb, po=po, xres=xres, bo=bo):
                s = slice(dc * T + tb * TBW, dc * T + tb * TBW + TBW)
                ins = v.scalar_tensor_tensor(xres[:, s], po[:, :], bo[:, dc:dc + 1],
                                             xres[:, s], op0=ALU.add, op1=ALU.add)
                ins.then_inc(ev, 1)
            ctx.inc("ev", 1)
            ctx.emit("vector", oev)
    return ctx.val("ev")


def s1_inputs(inp):
    return {
        "s1_wq": inp["L0_attn_a_Wq"],
        "s1_wk": inp["L0_attn_a_Wk"],
        "s1_wv": inp["L0_attn_a_Wv"],
        "s1_wo": inp["L0_attn_a_Wo"],
        "s1_bq": np.ascontiguousarray(inp["L0_attn_a_bq"].reshape(8, 128).T),
        "s1_bk": np.ascontiguousarray(
            np.pad(inp["L0_attn_a_bk"], (0, 64)).reshape(1, 128).T),
        "s1_bv": np.ascontiguousarray(
            np.pad(inp["L0_attn_a_bv"], (0, 64)).reshape(1, 128).T),
        "s1_bo": np.ascontiguousarray(inp["L0_attn_a_bo"].reshape(8, 128).T),
    }




def emit_moe(ctx, x_d, xo_d, dep_ev, tag):
    nc = ctx.nc
    E, HH, HC = 8, 2048, 16
    wred_d = ctx.dram("moe_wred", [D, 16])
    wg_d = ctx.dram("moe_wg", [8, 16])
    w1_d = ctx.dram("moe_w1", [E, D, HH])
    b1_d = ctx.dram("moe_b1", [128, E * HC])
    w2_d = ctx.dram("moe_w2", [E, HH, D])
    b2_d = ctx.dram("moe_b2", [1, E * D])

    y = ctx.sb("sc_big_y", [128, FC * T], BF16)
    wsb = ctx.sb("sc_wmoe", [128, 16384], BF16)
    acc = ctx.sb("sc_acc", [128, FC * T], BF16)
    ghA = ctx.sb("sc_xb", [128, FC * TBW], BF16)
    ghB = ctx.sb("sc_sq", [128, FC * TBW], BF16)
    wredsb = ctx.sb("sc_wred", [128, FC * 16], BF16)
    wgsb = ctx.sb("sc_wg", [8, 16], F32)
    wgsq = ctx.sb("sc_wgsq", [8, 16], F32)
    wgnb = ctx.sb("sc_wgn", [8, 16], BF16)
    wgnT = ctx.sb("sc_wgnT", [16, 8], BF16)
    xr = ctx.sb("sc_xr", [16, T], BF16)
    lg = ctx.sb("sc_lg", [8, T], F32)
    ltm = ctx.sb("sc_ltm", [128, 16], F32)
    cols = ctx.sb("sc_cols", [128, 4], F32)
    sccol = ctx.sb("sc_sccol", [128, 16], F32)
    idcol = ctx.sb("sc_idcol", [128, 16], F32)
    mu8 = ctx.sb("sc_mu8", [128, 8], U8)
    iot = ctx.sb("sc_iota8", [128, 8], F32)
    big8 = ctx.sb("sc_big8", [128, 8], F32)
    rowSI = ctx.sb("sc_rowSI", [1, 2 * T], F32)
    ohrow = ctx.sb("sc_oh", [1, T], BF16)
    nmrow = ctx.sb("sc_nm", [1, T], BF16)
    b1sb = ctx.sb("sc_b1", [128, E * HC], F32)
    b2sb = ctx.sb("sc_b2", [1, E * D], BF16)
    sb512 = ctx.sb("sc_s512", [128, TBW], F32)
    xin = ctx.sb("sc_xin", [128, T], F32)

    pln0 = ctx.ps("pln0", [1, TBW])
    p512a = ctx.ps("p512a", [128, TBW])
    p512b = ctx.ps("p512b", [128, TBW])
    p512c = ctx.ps("p512c", [128, TBW])

    pmisc = ctx.ps("pmisc", [128, 16], BF16)
    ptrf = ctx.ps("prow2", [1, 256], F32)
    prow2 = ptrf
    ones = ctx.sbuf["c_ones"]
    onesrow = ctx.sbuf["c_onesrow"]
    onesrowf = ctx.sbuf["c_onesrowf"]
    idb = ctx.sbuf["c_idb"]
    idf = ctx.sbuf["c_idf"]
    mm = ctx.sem("mm"); ev = ctx.sem("ev"); ac = ctx.sem("ac"); gp = ctx.sem("gp")

    wl0 = emit_dma(ctx, "gpsimd",
                   wredsb.ap().rearrange("p (fc g) -> p fc g", g=16),
                   wred_d.rearrange("(fc p) g -> p fc g", p=128), "wl")
    emit_dma(ctx, "gpsimd", wgsb[:, :], wg_d[:, :], "wl")
    emit_dma(ctx, "gpsimd", b1sb[:, :], b1_d[:, :], "wl")
    wl1 = emit_dma(ctx, "gpsimd", b2sb[:, :], b2_d[:, :], "wl")
    def mkiota(g, iot=iot):
        for i in range(8):
            g.memset(iot[:, i:i + 1], float(i))
        g.memset(big8[:, :], 99.0).then_inc(gp, 1)
    gpi = ctx.inc("gp", 1)
    ctx.emit("gpsimd", mkiota)

    ln_ev = emit_layernorm(ctx, tag, None, y, dep=("ev", dep_ev), x_dram=x_d)

    emit_wait(ctx, "tensor", "ev", ln_ev)
    emit_wait(ctx, "tensor", "wl", wl1)
    for tb in range(TB):
        if tb > 0:
            emit_wait(ctx, "tensor", "ev", ctx.val("ev"))
        def xrm(t, tb=tb, wredsb=wredsb, y=y, p16=p512a):
            for fc in range(FC):
                ins = t.matmul(p16[0:16, :], wredsb[:, fc * 16:(fc + 1) * 16],
                               y[:, fc * T + tb * TBW:fc * T + tb * TBW + TBW],
                               start=(fc == 0), stop=(fc == FC - 1))
            ins.then_inc(mm, 1)
        mv = ctx.inc("mm", 1)
        ctx.emit("tensor", xrm)
        emit_wait(ctx, "vector", "mm", mv)
        def xre(v, tb=tb, p16=p512a, xr=xr):
            ins = v.tensor_copy(xr[0:16, tb * TBW:tb * TBW + TBW], p16[0:16, :])
            ins.then_inc(ev, 1)
        ctx.inc("ev", 1)
        ctx.emit("vector", xre)
    xr_ev = ctx.val("ev")
    emit_wait(ctx, "scalar", "wl", wl1)
    def wsq(s, wgsb=wgsb, wgsq=wgsq):
        ins = s.activation(out=wgsq[0:8, :], in_=wgsb[0:8, :], func=AF.Square)
        ins.then_inc(ac, 1)
    a1 = ctx.inc("ac", 1)
    ctx.emit("scalar", wsq)
    emit_wait(ctx, "vector", "ac", a1)
    def wn1(v, wgsq=wgsq, cols=cols):
        ins = v.reduce_sum(cols[0:8, 0:1], wgsq[0:8, :], axis=AX.X)
        ins.then_inc(ev, 1)
    v1 = ctx.inc("ev", 1)
    ctx.emit("vector", wn1)
    emit_wait(ctx, "scalar", "ev", v1)
    def wn2(s, cols=cols):
        s.activation(out=cols[0:8, 1:2], in_=cols[0:8, 0:1], func=AF.Ln)
        ins = s.activation(out=cols[0:8, 2:3], in_=cols[0:8, 1:2], func=AF.Exp,
                           scale=0.5)
        ins.then_inc(ac, 1)
    a2 = ctx.inc("ac", 1)
    ctx.emit("scalar", wn2)
    emit_wait(ctx, "vector", "ac", a2)
    def wn3a(v, cols=cols):
        v.tensor_scalar_max(cols[0:8, 2:3], cols[0:8, 2:3], 1e-4)
        ins = v.reciprocal(cols[0:8, 3:4], cols[0:8, 2:3])
        ins.then_inc(ev, 1)
    v2a = ctx.inc("ev", 1)
    ctx.emit("vector", wn3a)
    emit_wait(ctx, "vector", "ev", v2a)
    def wn3b(v, cols=cols, wgsb=wgsb, wgnb=wgnb):
        ins = v.tensor_scalar(wgnb[0:8, :], wgsb[0:8, :], cols[0:8, 3:4], None,
                              op0=ALU.mult)
        ins.then_inc(ev, 1)
    v2 = ctx.inc("ev", 1)
    ctx.emit("vector", wn3b)
    emit_wait(ctx, "tensor", "ev", v2)
    def wgt(t, wgnb=wgnb, pmisc=pmisc, idb=idb):
        ins = t.transpose(pmisc[0:16, 0:8], wgnb[0:8, :], idb[0:8, 0:8])
        ins.then_inc(mm, 1)
    mt = ctx.inc("mm", 1)
    ctx.emit("tensor", wgt)
    emit_wait(ctx, "vector", "mm", mt)
    def wgte(v, pmisc=pmisc, wgnT=wgnT):
        ins = v.tensor_copy(wgnT[0:16, 0:8], pmisc[0:16, 0:8])
        ins.then_inc(ev, 1)
    v3 = ctx.inc("ev", 1)
    ctx.emit("vector", wgte)
    emit_wait(ctx, "tensor", "ev", max(v3, xr_ev))
    for tb in range(TB):
        if tb > 0:
            emit_wait(ctx, "tensor", "ev", ctx.val("ev"))
        def lgm(t, tb=tb, wgnT=wgnT, xr=xr, p16=p512a):
            ins = t.matmul(p16[0:8, :], wgnT[0:16, 0:8],
                           xr[0:16, tb * TBW:tb * TBW + TBW], start=True, stop=True)
            ins.then_inc(mm, 1)
        ml = ctx.inc("mm", 1)
        ctx.emit("tensor", lgm)
        emit_wait(ctx, "vector", "mm", ml)
        def lge(v, tb=tb, p16=p512a, lg=lg):
            ins = v.tensor_copy(lg[0:8, tb * TBW:tb * TBW + TBW], p16[0:8, :])
            ins.then_inc(ev, 1)
        ctx.inc("ev", 1)
        ctx.emit("vector", lge)
    lg_ev = ctx.val("ev")
    emit_wait(ctx, "tensor", "ev", lg_ev)
    emit_wait(ctx, "vector", "gp", gpi)
    for tc in range(16):
        if tc > 0:
            emit_wait(ctx, "tensor", "ev", ctx.val("ev"))
        def ltr(t, tc=tc, lg=lg, pltr=p512b, idf=idf):
            ins = t.transpose(pltr[:, 0:8], lg[0:8, tc * 128:(tc + 1) * 128],
                              idf[0:8, 0:8])
            ins.then_inc(mm, 1)
        mtr = ctx.inc("mm", 1)
        ctx.emit("tensor", ltr)
        emit_wait(ctx, "vector", "mm", mtr)
        emit_wait(ctx, "vector", "ac", ctx.val("ac"))
        def sfm(v, tc=tc, pltr=p512b, ltm=ltm, cols=cols, ev=ev,
                base=ctx.val("ev")):
            v.tensor_copy(ltm[:, 0:8], pltr[:, 0:8]).then_inc(ev, 1)
            v.wait_ge(ev, base + 1)
            v.reduce_max(cols[:, 0:1], ltm[:, 0:8], axis=AX.X).then_inc(ev, 1)
            v.wait_ge(ev, base + 2)
            ins = v.tensor_scalar_mul(cols[:, 1:2], cols[:, 0:1], -1.0)
            ins.then_inc(ev, 1)
        ctx.inc("ev", 2)
        vv = ctx.inc("ev", 1)
        ctx.emit("vector", sfm)
        emit_wait(ctx, "scalar", "ev", vv)
        def sfe(s, ltm=ltm, cols=cols):
            ins = s.activation(out=ltm[:, 8:16], in_=ltm[:, 0:8], func=AF.Exp,
                               bias=cols[:, 1:2])
            ins.then_inc(ac, 1)
        aa = ctx.inc("ac", 1)
        ctx.emit("scalar", sfe)
        emit_wait(ctx, "vector", "ac", aa)
        def arg(v, tc=tc, ltm=ltm, cols=cols, iot=iot,
                sccol=sccol, idcol=idcol, ev=ev, base=ctx.val("ev")):
            v.reduce_sum(cols[:, 2:3], ltm[:, 8:16], axis=AX.X).then_inc(ev, 1)
            v.wait_ge(ev, base + 1)
            v.reciprocal(sccol[:, tc:tc + 1], cols[:, 2:3])
            v.tensor_scalar(ltm[:, 8:16], ltm[:, 0:8], cols[:, 0:1], None,
                            op0=ALU.is_ge).then_inc(ev, 1)
            v.wait_ge(ev, base + 2)
            v.scalar_tensor_tensor(ltm[:, 8:16], ltm[:, 8:16], -99.0,
                                   iot[:, 0:8], op0=ALU.mult, op1=ALU.add).then_inc(ev, 1)
            v.wait_ge(ev, base + 3)
            ins = v.tensor_reduce(idcol[:, tc:tc + 1], ltm[:, 8:16], axis=AX.X,
                                  op=ALU.min)
            ins.then_inc(ev, 1)
        ctx.inc("ev", 4)
        ctx.emit("vector", arg)
    arg_ev = ctx.val("ev")
    scid2 = ctx.sb("sc_scid2", [128, 2], F32)
    for tc in range(16):
        def cpk(v, tc=tc, sccol=sccol, idcol=idcol, scid2=scid2):
            v.tensor_copy(scid2[:, 0:1], sccol[:, tc:tc + 1])
            ins = v.tensor_copy(scid2[:, 1:2], idcol[:, tc:tc + 1])
            ins.then_inc(ev, 1)
        vc = ctx.inc("ev", 1)
        ctx.emit("vector", cpk)
        emit_wait(ctx, "tensor", "ev", vc)
        def ctr3(t, scid2=scid2, prow2=prow2, idf=idf):
            t.transpose(prow2[0:1, 0:128], scid2[:, 0:1], idf[:, :])
            ins = t.transpose(prow2[0:1, 128:256], scid2[:, 1:2], idf[:, :])
            ins.then_inc(mm, 1)
        mc = ctx.inc("mm", 1)
        ctx.emit("tensor", ctr3)
        emit_wait(ctx, "vector", "mm", mc)
        def cre(v, tc=tc, prow2=prow2, rowSI=rowSI):
            v.tensor_copy(rowSI[0:1, tc * 128:(tc + 1) * 128], prow2[0:1, 0:128])
            ins = v.tensor_copy(rowSI[0:1, T + tc * 128:T + (tc + 1) * 128],
                                prow2[0:1, 128:256])
            ins.then_inc(ev, 1)
        ctx.inc("ev", 1)
        ctx.emit("vector", cre)
    row_ev = ctx.val("ev")
    oh_ev = row_ev

    def w1ap(e, half):
        return w1_d[e].rearrange("(fc p) h -> p fc h", p=128)[:, :, half * 1024:(half + 1) * 1024]
    def w2ap(e, half):
        return w2_d[e].rearrange("(hc p) d -> p hc d", p=128)[:, half * 8:(half + 1) * 8, :]
    w1h = wsb.ap().rearrange("p (fc h) -> p fc h", h=1024)
    w1hs = wsb
    w2base = 8192

    emit_wait(ctx, "tensor", "ev", oh_ev)
    for e in range(E):
        emit_wait(ctx, "vector", "mm", ctx.val("mm"))
        def ohx(v, e=e, rowSI=rowSI, ohrow=ohrow, nmrow=nmrow):
            v.tensor_single_scalar(ohrow[0:1, :], rowSI[0:1, T:2 * T],
                                   float(e) - 99.0, ALU.is_equal)
            ins = v.tensor_scalar(nmrow[0:1, :], ohrow[0:1, :], -1.0, 16384.0,
                                  op0=ALU.add, op1=ALU.mult)
            ins.then_inc(ev, 1)
        ohe = ctx.inc("ev", 1)
        ctx.emit("vector", ohx)
        emit_wait(ctx, "tensor", "ev", ohe)
        for half in range(2):
            emit_wait(ctx, "gpsimd", "mm", ctx.val("mm"))
            wle1 = emit_dma(ctx, "gpsimd",
                            wsb.ap().rearrange("p (fc h) -> p fc h", h=1024)[:, 0:8, :],
                            w1ap(e, half), "wl")
            wle2 = emit_dma(ctx, "gpsimd",
                            wsb.ap().rearrange("p (hc d) -> p hc d", d=D)[:, 8:16, :],
                            w2ap(e, half), "wl")
            emit_wait(ctx, "tensor", "wl", wle2)
            for tb in range(TB):
                for hc in range(8):
                    hcg = half * 8 + hc
                    ph = p512a if hc % 2 == 0 else p512b
                    emit_wait(ctx, "tensor", "ac", ctx.val("ac"))
                    def w1m(t, e=e, tb=tb, hc=hc, ph=ph, wsb=wsb, y=y,
                            onesrow=onesrow, nmrow=nmrow):
                        for fc in range(FC):
                            t.matmul(ph[:, :],
                                     wsb[:, fc * 1024 + hc * 128:fc * 1024 + hc * 128 + 128],
                                     y[:, fc * T + tb * TBW:fc * T + tb * TBW + TBW],
                                     start=(fc == 0), stop=False)
                        ins = t.matmul(ph[:, :], onesrow[0:1, :],
                                       nmrow[0:1, tb * TBW:tb * TBW + TBW],
                                       start=False, stop=True)
                        ins.then_inc(mm, 1)
                    mw = ctx.inc("mm", 1)
                    ctx.emit("tensor", w1m)
                    emit_wait(ctx, "scalar", "mm", mw)
                    def gel(s, e=e, hc=hc, hcg=hcg, ph=ph, b1sb=b1sb,
                            ghx=(ghA if hc < 8 else ghB)):
                        ins = s.activation(out=ghx[:, (hc % 8) * TBW:(hc % 8) * TBW + TBW],
                                           in_=ph[:, :], func=AF.Gelu,
                                           bias=b1sb[:, e * HC + hcg:e * HC + hcg + 1])
                        ins.then_inc(ac, 1)
                    ctx.inc("ac", 1)
                    ctx.emit("scalar", gel)
                emit_wait(ctx, "tensor", "ac", ctx.val("ac"))
                for dc in range(FC):
                    po = p512c
                    emit_wait(ctx, "tensor", "ev", ctx.val("ev"))
                    def w2m(t, e=e, tb=tb, dc=dc, half=half, po=po, wsb=wsb,
                            b2sb=b2sb, ohrow=ohrow, ghA=ghA):
                        for hc in range(8):
                            ins = t.matmul(po[:, :],
                                     wsb[:, w2base + hc * D + dc * 128:w2base + hc * D + dc * 128 + 128],
                                     ghA[:, hc * TBW:hc * TBW + TBW],
                                     start=(hc == 0),
                                     stop=(half == 1 and hc == 7))
                        if half == 0:
                            ins = t.matmul(po[:, :],
                                           b2sb[0:1, e * D + dc * 128:e * D + dc * 128 + 128],
                                           ohrow[0:1, tb * TBW:tb * TBW + TBW],
                                           start=False, stop=True)
                        ins.then_inc(mm, 1)
                    mw2 = ctx.inc("mm", 1)
                    ctx.emit("tensor", w2m)
                    emit_wait(ctx, "vector", "mm", mw2)
                    def acd(v, e=e, tb=tb, dc=dc, half=half, po=po, acc=acc):
                        s = slice(dc * T + tb * TBW, dc * T + tb * TBW + TBW)
                        if e == 0 and half == 0:
                            ins = v.tensor_copy(acc[:, s], po[:, :])
                        else:
                            ins = v.tensor_tensor(acc[:, s], acc[:, s], po[:, :], ALU.add)
                        ins.then_inc(ev, 1)
                    ctx.inc("ev", 1)
                    ctx.emit("vector", acd)

    acc_ev = ctx.val("ev")
    emit_wait(ctx, "tensor", "ev", acc_ev)
    sts = ctx.sem("st")
    for tb in range(TB):
        emit_wait(ctx, "tensor", "ev", ctx.val("ev"))
        def scb(t, tb=tb, rowSI=rowSI, p512a=p512a, onesrowf=onesrowf):
            ins = t.matmul(p512a[:, :], onesrowf[0:1, :],
                           rowSI[0:1, tb * TBW:tb * TBW + TBW], start=True, stop=True)
            ins.then_inc(mm, 1)
        msb = ctx.inc("mm", 1)
        ctx.emit("tensor", scb)
        emit_wait(ctx, "vector", "mm", msb)
        def sce(v, tb=tb, p512a=p512a, acc=acc, sb512=sb512):
            v.tensor_copy(sb512[:, :], p512a[:, :])
            for dc in range(FC):
                s = slice(dc * T + tb * TBW, dc * T + tb * TBW + TBW)
                ins = v.tensor_tensor(acc[:, s], acc[:, s], sb512[:, :], ALU.mult)
            ins.then_inc(ev, 1)
        ctx.inc("ev", 1)
        ctx.emit("vector", sce)
    sc_ev = ctx.val("ev")
    for dc in range(FC):
        if dc > 0:
            emit_wait(ctx, "gpsimd", "st", ctx.val("st"))
        lx = emit_dma(ctx, "gpsimd", xin[:, :],
                      x_d[dc * 128:(dc + 1) * 128, :], "ld")
        emit_wait(ctx, "vector", "ld", lx)
        emit_wait(ctx, "vector", "ev", sc_ev)
        if dc > 0:
            emit_wait(ctx, "vector", "st", ctx.val("st"))
        def fin(v, dc=dc, acc=acc, xin=xin):
            ins = v.tensor_tensor(xin[:, :], xin[:, :],
                                  acc[:, dc * T:(dc + 1) * T], ALU.add)
            ins.then_inc(ev, 1)
        vf = ctx.inc("ev", 1)
        ctx.emit("vector", fin)
        emit_wait(ctx, "sync", "ev", vf)
        emit_dma(ctx, "sync", xo_d[dc * 128:(dc + 1) * 128, :], xin[:, :], "st")
    emit_wait(ctx, "sync", "st", ctx.val("st"))
    return ctx.val("ev")





def emit_s3(ctx, x_d, xo_d, dep_ev, tag="s3"):
    nc = ctx.nc
    H, DK = 16, 64
    wq_d = ctx.dram("s3_wqT", [D, D])
    wk_d = ctx.dram("s3_wkT", [D, D])
    wv_d = ctx.dram("s3_wvT", [D, D])
    wo_d = ctx.dram("s3_woT", [D, D])

    y = ctx.sb("sc_big_y", [128, FC * T], BF16)
    osb = ctx.sb("sc_osb", [128, FC * T], BF16)
    qsb = ctx.sb("sc_qg", [128, 4 * T], BF16)
    ksb = ctx.sb("sc_kg", [128, 4 * T], BF16)
    wsb = ctx.sb("sc_w", [128, FC * D], BF16)
    wvsb = ctx.sb("sc_wv3", [128, FC * D], BF16)
    vh = ctx.sb("sc_vh", [128, 16 * 65], BF16)
    ptile = ctx.sb("sc_pt3", [128, 16 * 128], BF16)
    binv = ctx.sb("sc_binv", [64, TBW], BF16)
    rrow = ctx.sb("sc_rrow", [1, TBW], F32)
    nrows = ctx.sb("sc_nr3", [1, 256], F32)
    ccol = ctx.sb("sc_cc3", [128, 16], F32)
    crow = ctx.sb("sc_crow", [1, 16], F32)
    mask = ctx.sb("sc_mask", [128, 128], F32)
    xin = ctx.sb("sc_xin", [128, T], F32)

    pln0 = ctx.ps("pln0", [1, TBW])
    p512a = ctx.ps("p512a", [128, TBW])
    p512b = ctx.ps("p512b", [128, TBW])
    p512c = ctx.ps("p512c", [128, TBW])
    pav = ctx.ps("pav", [65, TBW])
    prow = ctx.ps("prow", [1, TBW])
    ones = ctx.sbuf["c_ones"]
    onesrow = ctx.sbuf["c_onesrow"]
    onesrowf = ctx.sbuf["c_onesrowf"]
    mm = ctx.sem("mm"); ev = ctx.sem("ev"); ac = ctx.sem("ac"); gp = ctx.sem("gp")
    sts = ctx.sem("st")

    def wap(w_dram):
        return w_dram.rearrange("(fc p) g -> p fc g", p=128)
    def w3(sb_t, gw):
        return sb_t.ap().rearrange("p (fc g) -> p fc g", g=gw)

    def mkmask(g, mask=mask):
        g.memset(mask[:, :], 0.0)
        g.affine_select(out=mask[:, :], in_=mask[:, :], compare_op=ALU.is_ge,
                        fill=-1e9, base=0, pattern=[[1, 128]],
                        channel_multiplier=-1).then_inc(gp, 1)
    gpm = ctx.inc("gp", 1)
    ctx.emit("gpsimd", mkmask)

    wlv = emit_dma(ctx, "gpsimd", w3(wvsb, D), wap(wv_d), "wl")

    ln_ev = emit_layernorm(ctx, tag, None, y, dep=("ev", dep_ev), x_dram=x_d)

    att_done = ln_ev
    for hg in range(2):
        emit_wait(ctx, "gpsimd", "mm", ctx.val("mm"))
        wlq = emit_dma(ctx, "gpsimd", w3(wsb, D), wap(wq_d), "wl")
        emit_wait(ctx, "tensor", "wl", wlq)
        emit_wait(ctx, "tensor", "ev", ctx.val("ev"))
        for gc in range(4):
            gg = hg * 4 + gc
            for tb in range(TB):
                pq = p512a if (gc * TB + tb) % 2 == 0 else p512b
                emit_wait(ctx, "tensor", "ev", ctx.val("ev"))
                def qmm(t, gg=gg, tb=tb, pq=pq, wsb=wsb, y=y):
                    for fc in range(FC):
                        ins = t.matmul(pq[:, :],
                                       wsb[:, fc * D + gg * 128:fc * D + gg * 128 + 128],
                                       y[:, fc * T + tb * TBW:fc * T + tb * TBW + TBW],
                                       start=(fc == 0), stop=(fc == FC - 1))
                    ins.then_inc(mm, 1)
                mq = ctx.inc("mm", 1)
                ctx.emit("tensor", qmm)
                emit_wait(ctx, "vector", "mm", mq)
                def qev(v, gc=gc, tb=tb, pq=pq, qsb=qsb):
                    ins = v.tensor_copy(qsb[:, gc * T + tb * TBW:gc * T + tb * TBW + TBW],
                                        pq[:, :])
                    ins.then_inc(ev, 1)
                ctx.inc("ev", 1)
                ctx.emit("vector", qev)
        emit_wait(ctx, "gpsimd", "mm", ctx.val("mm"))
        wlk = emit_dma(ctx, "gpsimd", w3(wsb, D), wap(wk_d), "wl")
        emit_wait(ctx, "tensor", "wl", wlk)
        for gc in range(4):
            gg = hg * 4 + gc
            for tb in range(TB):
                pq = p512a if (gc * TB + tb) % 2 == 0 else p512b
                emit_wait(ctx, "tensor", "ev", ctx.val("ev"))
                def kmm(t, gg=gg, tb=tb, pq=pq, wsb=wsb, y=y):
                    for fc in range(FC):
                        ins = t.matmul(pq[:, :],
                                       wsb[:, fc * D + gg * 128:fc * D + gg * 128 + 128],
                                       y[:, fc * T + tb * TBW:fc * T + tb * TBW + TBW],
                                       start=(fc == 0), stop=(fc == FC - 1))
                    ins.then_inc(mm, 1)
                mk = ctx.inc("mm", 1)
                ctx.emit("tensor", kmm)
                emit_wait(ctx, "vector", "mm", mk)
                def kev(v, gc=gc, tb=tb, pq=pq, ksb=ksb):
                    ins = v.tensor_copy(ksb[:, gc * T + tb * TBW:gc * T + tb * TBW + TBW],
                                        pq[:, :])
                    ins.then_inc(ev, 1)
                ctx.inc("ev", 1)
                ctx.emit("vector", kev)
        qk_ev = ctx.val("ev")
        sqh = ctx.sb("sc_sq", [128, FC * TBW], BF16)
        emit_wait(ctx, "scalar", "ev", qk_ev)
        for h8 in range(8):
            h = hg * 8 + h8
            hb = (h8 % 2) * 64
            gc = h8 // 2
            for src_, off in ((qsb, 0), (ksb, 8)):
                for tb in range(TB):
                    emit_wait(ctx, "scalar", "mm", ctx.val("mm"))
                    def sqq(s, src_=src_, gc=gc, hb=hb, tb=tb, sqh=sqh):
                        ins = s.activation(out=sqh[hb:hb + 64, 0:TBW],
                                           in_=src_[hb:hb + 64, gc * T + tb * TBW:gc * T + tb * TBW + TBW],
                                           func=AF.Square)
                        ins.then_inc(ac, 1)
                    a1 = ctx.inc("ac", 1)
                    ctx.emit("scalar", sqq)
                    emit_wait(ctx, "tensor", "ac", a1)
                    emit_wait(ctx, "tensor", "ev", ctx.val("ev"))
                    def nmm(t, hb=hb, sqh=sqh, prow=prow, ones=ones):
                        ins = t.matmul(prow[0:1, :], ones[hb:hb + 64, 0:1],
                                       sqh[hb:hb + 64, 0:TBW], start=True, stop=True)
                        ins.then_inc(mm, 1)
                    m1 = ctx.inc("mm", 1)
                    ctx.emit("tensor", nmm)
                    emit_wait(ctx, "vector", "mm", m1)
                    def nre(v, h8=h8, off=off, tb=tb, prow=prow, nrows=nrows):
                        ins = v.reduce_max(nrows[0:1, h8 * 16 + off + tb:h8 * 16 + off + tb + 1],
                                           prow[0:1, :], axis=AX.X)
                        ins.then_inc(ev, 1)
                    ctx.inc("ev", 1)
                    ctx.emit("vector", nre)
        nv = ctx.val("ev")
        emit_wait(ctx, "vector", "ev", nv)
        def cmax(v, hg=hg, nrows=nrows, ev=ev, base=ctx.val("ev")):
            for h8 in range(8):
                v.reduce_max(nrows[0:1, 128 + h8 * 2:128 + h8 * 2 + 1],
                             nrows[0:1, h8 * 16:h8 * 16 + 4], axis=AX.X)
                ins = v.reduce_max(nrows[0:1, 128 + h8 * 2 + 1:128 + h8 * 2 + 2],
                                   nrows[0:1, h8 * 16 + 8:h8 * 16 + 12], axis=AX.X)
            ins.then_inc(ev, 1)
        vcm = ctx.inc("ev", 1)
        ctx.emit("vector", cmax)
        emit_wait(ctx, "scalar", "ev", vcm)
        def cln(s, nrows=nrows):
            ins = s.activation(out=nrows[0:1, 144:160], in_=nrows[0:1, 128:144],
                               func=AF.Ln)
            ins.then_inc(ac, 1)
        a2 = ctx.inc("ac", 1)
        ctx.emit("scalar", cln)
        emit_wait(ctx, "vector", "ac", a2)
        def csum(v, nrows=nrows, ev=ev, base=ctx.val("ev")):
            ins = v.tensor_add(nrows[0:1, 160:168].rearrange("a b -> a b"),
                               nrows[0:1, 144:160].rearrange("a (h two) -> a h two", two=2)[:, :, 0],
                               nrows[0:1, 144:160].rearrange("a (h two) -> a h two", two=2)[:, :, 1])
            ins.then_inc(ev, 1)
        vcs = ctx.inc("ev", 1)
        ctx.emit("vector", csum)
        emit_wait(ctx, "scalar", "ev", vcs)
        def cexp(s, nrows=nrows):
            ins = s.activation(out=nrows[0:1, 168:176], in_=nrows[0:1, 160:168],
                               func=AF.Exp, scale=0.5)
            ins.then_inc(ac, 1)
        a3 = ctx.inc("ac", 1)
        ctx.emit("scalar", cexp)
        emit_wait(ctx, "vector", "ac", a3)
        def cneg(v, hg=hg, nrows=nrows, crow=crow):
            ins = v.tensor_scalar_mul(crow[0:1, hg * 8:hg * 8 + 8],
                                      nrows[0:1, 168:176], -0.125)
            ins.then_inc(ev, 1)
        vcn = ctx.inc("ev", 1)
        ctx.emit("vector", cneg)
        emit_wait(ctx, "tensor", "ev", vcn)
        def cbc(t, hg=hg, crow=crow, p512c=p512c, onesrowf=onesrowf):
            ins = t.matmul(p512c[:, 0:8], onesrowf[0:1, :],
                           crow[0:1, hg * 8:hg * 8 + 8], start=True, stop=True)
            ins.then_inc(mm, 1)
        mcb = ctx.inc("mm", 1)
        ctx.emit("tensor", cbc)
        emit_wait(ctx, "vector", "mm", mcb)
        def cbe(v, hg=hg, p512c=p512c, ccol=ccol):
            ins = v.tensor_copy(ccol[:, hg * 8:hg * 8 + 8], p512c[:, 0:8])
            ins.then_inc(ev, 1)
        vbe = ctx.inc("ev", 1)
        ctx.emit("vector", cbe)
        emit_wait(ctx, "tensor", "ev", vbe)
        emit_wait(ctx, "tensor", "gp", gpm)
        emit_wait(ctx, "tensor", "wl", wlv)

        for h8 in range(8):
            h = hg * 8 + h8
            hb = (h8 % 2) * 64
            gc = h8 // 2
            for tc in range(16):
                emit_wait(ctx, "tensor", "ev", ctx.val("ev"))
                def vmm(t, h=h, tc=tc, wvsb=wvsb, y=y, p=p512b, onesrow=onesrow):
                    for fc in range(FC):
                        ins = t.matmul(p[:, 0:64],
                                       y[:, fc * T + tc * 128:fc * T + tc * 128 + 128],
                                       wvsb[:, fc * D + h * 64:fc * D + h * 64 + 64],
                                       start=(fc == 0), stop=(fc == FC - 1))
                    ins.then_inc(mm, 1)
                mv = ctx.inc("mm", 1)
                ctx.emit("tensor", vmm)
                emit_wait(ctx, "vector", "mm", mv)
                def vev(v, tc=tc, p=p512b, vh=vh):
                    v.tensor_copy(vh[:, tc * 65:tc * 65 + 64], p[:, 0:64])
                    ins = v.memset(vh[:, tc * 65 + 64:tc * 65 + 65], 1.0)
                    ins.then_inc(ev, 1)
                ctx.inc("ev", 1)
                ctx.emit("vector", vev)
            for qt in range(16):
                nkt = qt + 1
                for kt in range(nkt):
                    pS = p512a if kt % 2 == 0 else p512b
                    emit_wait(ctx, "tensor", "ac", ctx.val("ac"))
                    emit_wait(ctx, "tensor", "ev", ctx.val("ev"))
                    def smm(t, hb=hb, gc=gc, qt=qt, kt=kt, pS=pS, ksb=ksb, qsb=qsb):
                        ins = t.matmul(pS[:, 0:128],
                                       ksb[hb:hb + 64, gc * T + kt * 128:gc * T + kt * 128 + 128],
                                       qsb[hb:hb + 64, gc * T + qt * 128:gc * T + qt * 128 + 128],
                                       start=True, stop=True)
                        ins.then_inc(mm, 1)
                    ms = ctx.inc("mm", 1)
                    ctx.emit("tensor", smm)
                    if kt == qt:
                        emit_wait(ctx, "vector", "mm", ms)
                        def madd(v, pS=pS, mask=mask):
                            ins = v.tensor_tensor(pS[:, 0:128], pS[:, 0:128],
                                                  mask[:, :], ALU.add)
                            ins.then_inc(ev, 1)
                        mev = ctx.inc("ev", 1)
                        ctx.emit("vector", madd)
                        emit_wait(ctx, "scalar", "ev", mev)
                    else:
                        emit_wait(ctx, "scalar", "mm", ms)
                    def sexp(s, h=h, kt=kt, pS=pS, ptile=ptile, ccol=ccol):
                        ins = s.activation(out=ptile[:, kt * 128:kt * 128 + 128],
                                           in_=pS[:, 0:128], func=AF.Exp,
                                           scale=0.125, bias=ccol[:, h:h + 1])
                        ins.then_inc(ac, 1)
                    ctx.inc("ac", 1)
                    ctx.emit("scalar", sexp)
                emit_wait(ctx, "tensor", "ac", ctx.val("ac"))
                emit_wait(ctx, "tensor", "ev", ctx.val("ev"))
                def avm(t, qt=qt, nkt=nkt, pav=pav, vh=vh, ptile=ptile):
                    for kt in range(nkt):
                        ins = t.matmul(pav[:, 0:128], vh[:, kt * 65:kt * 65 + 65],
                                       ptile[:, kt * 128:kt * 128 + 128],
                                       start=(kt == 0), stop=(kt == nkt - 1))
                    ins.then_inc(mm, 1)
                mav = ctx.inc("mm", 1)
                ctx.emit("tensor", avm)
                emit_wait(ctx, "vector", "mm", mav)
                def rec(v, pav=pav, rrow=rrow):
                    ins = v.reciprocal(rrow[0:1, 0:128], pav[64:65, 0:128])
                    ins.then_inc(ev, 1)
                vrc = ctx.inc("ev", 1)
                ctx.emit("vector", rec)
                emit_wait(ctx, "tensor", "ev", vrc)
                def rbc(t, rrow=rrow, p512c=p512c, onesrowf=onesrowf):
                    ins = t.matmul(p512c[0:64, 0:128], onesrowf[0:1, 0:64],
                                   rrow[0:1, 0:128], start=True, stop=True)
                    ins.then_inc(mm, 1)
                mrb = ctx.inc("mm", 1)
                ctx.emit("tensor", rbc)
                emit_wait(ctx, "vector", "mm", mrb)
                gfc = hg * 4 + h8 // 2
                def div(v, hb=hb, gfc=gfc, qt=qt, pav=pav, p512c=p512c, binv=binv, osb=osb):
                    v.tensor_copy(binv[:, 0:128], p512c[0:64, 0:128])
                    ins = v.tensor_tensor(osb[hb:hb + 64, gfc * T + qt * 128:gfc * T + qt * 128 + 128],
                                          pav[0:64, 0:128], binv[:, 0:128], ALU.mult)
                    ins.then_inc(ev, 1)
                ctx.inc("ev", 1)
                ctx.emit("vector", div)
        att_done = ctx.val("ev")
    emit_wait(ctx, "gpsimd", "mm", ctx.val("mm"))
    wlo = emit_dma(ctx, "gpsimd", w3(wsb, D), wap(wo_d), "wl")
    emit_wait(ctx, "tensor", "ev", att_done)
    emit_wait(ctx, "tensor", "wl", wlo)
    for dc in range(FC):
        if dc > 0:
            emit_wait(ctx, "gpsimd", "st", ctx.val("st"))
        lx = emit_dma(ctx, "gpsimd", xin[:, :], x_d[dc * 128:(dc + 1) * 128, :], "ld")
        for tb in range(TB):
            po = p512a if tb % 2 == 0 else p512b
            emit_wait(ctx, "tensor", "ev", ctx.val("ev"))
            def omm(t, dc=dc, tb=tb, po=po, wsb=wsb, osb=osb):
                for fc in range(FC):
                    ins = t.matmul(po[:, :],
                                   wsb[:, fc * D + dc * 128:fc * D + dc * 128 + 128],
                                   osb[:, fc * T + tb * TBW:fc * T + tb * TBW + TBW],
                                   start=(fc == 0), stop=(fc == FC - 1))
                ins.then_inc(mm, 1)
            mo = ctx.inc("mm", 1)
            ctx.emit("tensor", omm)
            emit_wait(ctx, "vector", "mm", mo)
            emit_wait(ctx, "vector", "ld", lx)
            def oev(v, tb=tb, po=po, xin=xin):
                s = slice(tb * TBW, tb * TBW + TBW)
                ins = v.tensor_tensor(xin[:, s], xin[:, s], po[:, :], ALU.add)
                ins.then_inc(ev, 1)
            ctx.inc("ev", 1)
            ctx.emit("vector", oev)
        emit_wait(ctx, "sync", "ev", ctx.val("ev"))
        emit_dma(ctx, "sync", xo_d[dc * 128:(dc + 1) * 128, :], xin[:, :], "st")
    emit_wait(ctx, "sync", "st", ctx.val("st"))
    return ctx.val("ev")


def s3_inputs(inp):
    return {
        "s3_wqT": np.ascontiguousarray(inp["L0_attn_b_Wq"].T),
        "s3_wkT": np.ascontiguousarray(inp["L0_attn_b_Wk"].T),
        "s3_wvT": np.ascontiguousarray(inp["L0_attn_b_Wv"].T),
        "s3_woT": np.ascontiguousarray(inp["L0_attn_b_Wo"].T),
    }





def emit_ffn1(ctx, x_d, xo_d, dep_ev, tag):
    nc = ctx.nc
    HH = 4096
    w1_d = ctx.dram("f1_w1T", [D, HH])
    w2_d = ctx.dram("f1_w2T", [HH, D])
    b1_d = ctx.dram("f1_b1", [128, 32])
    b2_d = ctx.dram("f1_b2", [128, 8])
    sc_d = ctx.dram("f1_scale", [1, 1])

    y = ctx.sb("sc_big_y", [128, FC * T], BF16)
    gh = ctx.sb("sc_gh1", [128, 8 * 1024], BF16)
    acc = ctx.sb("sc_accf", [128, 8 * 1024], F32)
    w1sb = ctx.sb("sc_w", [128, FC * D], BF16)
    w2sb = ctx.sb("sc_w2f", [128, FC * D], BF16)
    b1sb = ctx.sb("sc_b1f", [128, 32], F32)
    b2sb = ctx.sb("sc_b2f", [128, 8], F32)
    sccol = ctx.sb("sc_scl", [1, 4], F32)
    evrow = ctx.sb("sc_evrow", [1, 8 * 1024], BF16)
    p1row = ctx.sb("sc_p1row", [1, 4 * T], BF16)
    t2row = None
    xin = ctx.sb("sc_xin", [128, 1024], F32)

    pln0 = ctx.ps("pln0", [1, TBW])
    p512a = ctx.ps("p512a", [128, TBW])
    p512b = ctx.ps("p512b", [128, TBW])
    p512c = ctx.ps("p512c", [128, TBW])
    prow = ctx.ps("prow", [1, TBW])
    prowb = ctx.ps("prowb", [1, TBW], BF16)
    onesrowf = ctx.sbuf["c_onesrowf"]
    idf = ctx.sbuf["c_idf"]
    idb = ctx.sbuf["c_idb"]
    mm = ctx.sem("mm"); ev = ctx.sem("ev"); ac = ctx.sem("ac")

    wlb = emit_dma(ctx, "gpsimd", b1sb[:, :], b1_d[:, :], "wl")
    emit_dma(ctx, "gpsimd", b2sb[:, :], b2_d[:, :], "wl")
    wls = emit_dma(ctx, "gpsimd", sccol[0:1, 0:1], sc_d[0:1, 0:1], "wl")

    ln_ev = emit_layernorm(ctx, tag, None, y, dep=("ev", dep_ev), x_dram=x_d)
    t2row = ctx.alias("sc_t2row", [1, 16 * 1024], BF16, ["sc_xt", "sc_xb", "sc_sq"])

    emit_wait(ctx, "tensor", "ev", ln_ev)
    for j, tok in enumerate([1, 3, 5, 7, 9, 11, 13, 15]):
        for fc in range(FC):
            emit_wait(ctx, "tensor", "ev", ctx.val("ev"))
            def etr(t, tok=tok, fc=fc, y=y, prowb=prowb, idb=idb):
                ins = t.transpose(prowb[0:1, 0:128],
                                  y[:, fc * T + tok:fc * T + tok + 1], idb[:, :])
                ins.then_inc(mm, 1)
            me = ctx.inc("mm", 1)
            ctx.emit("tensor", etr)
            emit_wait(ctx, "vector", "mm", me)
            def ecp(v, j=j, fc=fc, prowb=prowb, evrow=evrow):
                ins = v.tensor_copy(evrow[0:1, j * 1024 + fc * 128:j * 1024 + fc * 128 + 128],
                                    prowb[0:1, 0:128])
                ins.then_inc(ev, 1)
            ctx.inc("ev", 1)
            ctx.emit("vector", ecp)
    ev_ev = ctx.val("ev")

    emit_wait(ctx, "tensor", "ev", ev_ev)
    emit_wait(ctx, "vector", "wl", wls)
    for i in range(4):
        for tb in range(TB):
            emit_wait(ctx, "tensor", "ev", ctx.val("ev"))
            def p1m(t, i=i, tb=tb, y=y, prow=prow):
                for fc in range(FC):
                    ins = t.matmul(prow[0:1, :],
                                   y[:, fc * T + 2 + 2 * i:fc * T + 3 + 2 * i],
                                   y[:, fc * T + tb * TBW:fc * T + tb * TBW + TBW],
                                   start=(fc == 0), stop=(fc == FC - 1))
                ins.then_inc(mm, 1)
            m1 = ctx.inc("mm", 1)
            ctx.emit("tensor", p1m)
            emit_wait(ctx, "vector", "mm", m1)
            def p1e(v, i=i, tb=tb, prow=prow, p1row=p1row, sccol=sccol):
                ins = v.tensor_scalar(p1row[0:1, i * T + tb * TBW:i * T + tb * TBW + TBW],
                                      prow[0:1, :], sccol[0:1, 0:1], None, op0=ALU.mult)
                ins.then_inc(ev, 1)
            ctx.inc("ev", 1)
            ctx.emit("vector", p1e)

    emit_wait(ctx, "tensor", "wl", wlb)
    for th in range(2):
        t0 = th * 1024
        for hgrp in range(4):
            emit_wait(ctx, "gpsimd", "mm", ctx.val("mm"))
            wl1 = emit_dma(ctx, "gpsimd",
                           w1sb.ap().rearrange("p (fc g) -> p fc g", g=D),
                           w1_d.rearrange("(fc p) g -> p fc g", p=128)[:, :, hgrp * 1024:(hgrp + 1) * 1024],
                           "wl")
            wl2 = emit_dma(ctx, "gpsimd",
                           w2sb.ap().rearrange("p (hc g) -> p hc g", g=D)[:, 0:8, :],
                           w2_d.rearrange("(hc p) g -> p hc g", p=128)[:, hgrp * 8:hgrp * 8 + 8, :],
                           "wl")
            emit_wait(ctx, "tensor", "wl", wl2)
            for hc8 in range(8):
                hc = hgrp * 8 + hc8
                for tsb in range(2):
                    ph = p512a if (hc8 * 2 + tsb) % 2 == 0 else p512b
                    emit_wait(ctx, "tensor", "ac", ctx.val("ac"))
                    def w1m(t, hc8=hc8, hc=hc, t0=t0, tsb=tsb, ph=ph, w1sb=w1sb,
                            y=y, evrow=evrow, p1row=p1row):
                        c0 = t0 + tsb * TBW
                        for fc in range(FC):
                            t.matmul(ph[:, :],
                                     w1sb[:, fc * D + hc8 * 128:fc * D + hc8 * 128 + 128],
                                     y[:, fc * T + c0:fc * T + c0 + TBW],
                                     start=(fc == 0), stop=False)
                        i = hc // 8
                        uc = hc % 8
                        ins = t.matmul(ph[:, :],
                                       evrow[0:1, i * 1024 + uc * 128:i * 1024 + uc * 128 + 128],
                                       p1row[0:1, i * T + c0:i * T + c0 + TBW],
                                       start=False, stop=True)
                        ins.then_inc(mm, 1)
                    mw = ctx.inc("mm", 1)
                    ctx.emit("tensor", w1m)
                    emit_wait(ctx, "scalar", "mm", mw)
                    def gel(s, hc8=hc8, hc=hc, tsb=tsb, ph=ph, b1sb=b1sb, gh=gh):
                        ins = s.activation(out=gh[:, hc8 * 1024 + tsb * TBW:hc8 * 1024 + tsb * TBW + TBW],
                                           in_=ph[:, :], func=AF.Gelu,
                                           bias=b1sb[:, hc:hc + 1])
                        ins.then_inc(ac, 1)
                    ctx.inc("ac", 1)
                    ctx.emit("scalar", gel)
            gh_ac = ctx.val("ac")
            emit_wait(ctx, "tensor", "ac", gh_ac)
            c = hgrp
            for i in range(4):
                for tsb in range(2):
                    emit_wait(ctx, "tensor", "ev", ctx.val("ev"))
                    def t2m(t, i=i, tsb=tsb, y=y, gh=gh, prow=prow):
                        for vc in range(8):
                            ins = t.matmul(prow[0:1, :],
                                           y[:, vc * T + 10 + 2 * i:vc * T + 11 + 2 * i],
                                           gh[:, vc * 1024 + tsb * TBW:vc * 1024 + tsb * TBW + TBW],
                                           start=(vc == 0), stop=(vc == 7))
                        ins.then_inc(mm, 1)
                    m2 = ctx.inc("mm", 1)
                    ctx.emit("tensor", t2m)
                    emit_wait(ctx, "vector", "mm", m2)
                    def t2e(v, i=i, c=c, tsb=tsb, prow=prow, t2row=t2row, sccol=sccol):
                        k = (i * 4 + c) * 1024 + tsb * TBW
                        ins = v.tensor_scalar(t2row[0:1, k:k + TBW], prow[0:1, :],
                                              sccol[0:1, 0:1], None, op0=ALU.mult)
                        ins.then_inc(ev, 1)
                    ctx.inc("ev", 1)
                    ctx.emit("vector", t2e)
            for dc in range(FC):
                for tsb in range(2):
                    po = p512c
                    emit_wait(ctx, "tensor", "ev", ctx.val("ev"))
                    def w2m(t, dc=dc, tsb=tsb, po=po, w2sb=w2sb, gh=gh):
                        for hc8 in range(8):
                            ins = t.matmul(po[:, :],
                                           w2sb[:, hc8 * D + dc * 128:hc8 * D + dc * 128 + 128],
                                           gh[:, hc8 * 1024 + tsb * TBW:hc8 * 1024 + tsb * TBW + TBW],
                                           start=(hc8 == 0), stop=(hc8 == 7))
                        ins.then_inc(mm, 1)
                    mw2 = ctx.inc("mm", 1)
                    ctx.emit("tensor", w2m)
                    emit_wait(ctx, "vector", "mm", mw2)
                    def acd(v, dc=dc, tsb=tsb, hgrp=hgrp, po=po, acc=acc):
                        s = slice(dc * 1024 + tsb * TBW, dc * 1024 + tsb * TBW + TBW)
                        if hgrp == 0:
                            ins = v.tensor_copy(acc[:, s], po[:, :])
                        else:
                            ins = v.tensor_tensor(acc[:, s], acc[:, s], po[:, :], ALU.add)
                        ins.then_inc(ev, 1)
                    ctx.inc("ev", 1)
                    ctx.emit("vector", acd)
        for dc in range(FC):
            if dc > 0:
                emit_wait(ctx, "gpsimd", "st", ctx.val("st"))
            lx = emit_dma(ctx, "gpsimd", xin[:, :],
                          x_d[dc * 128:(dc + 1) * 128, t0:t0 + 1024], "ld")
            for tsb in range(2):
                po = p512a
                emit_wait(ctx, "tensor", "ev", ctx.val("ev"))
                def rkm(t, dc=dc, tsb=tsb, po=po, evrow=evrow, t2row=t2row):
                    i = dc // 2
                    qb = (dc % 2) * 128
                    for cc in range(4):
                        kk = (i * 4 + cc) * 1024 + tsb * TBW
                        ins = t.matmul(po[:, :],
                                       bass.AP(evrow, (4 + i) * 1024 + qb * 4 + cc,
                                               [[8192, 1], [4, 128]]),
                                       t2row[0:1, kk:kk + TBW],
                                       start=(cc == 0), stop=(cc == 3))
                    ins.then_inc(mm, 1)
                mrk = ctx.inc("mm", 1)
                ctx.emit("tensor", rkm)
                emit_wait(ctx, "vector", "mm", mrk)
                emit_wait(ctx, "vector", "ld", lx)
                def fin(v, dc=dc, tsb=tsb, po=po, acc=acc, xin=xin, b2sb=b2sb):
                    s = slice(tsb * TBW, tsb * TBW + TBW)
                    sa = slice(dc * 1024 + tsb * TBW, dc * 1024 + tsb * TBW + TBW)
                    v.tensor_tensor(xin[:, s], xin[:, s], po[:, :], ALU.add)
                    v.tensor_tensor(xin[:, s], xin[:, s], acc[:, sa], ALU.add)
                    ins = v.tensor_scalar_add(xin[:, s], xin[:, s], b2sb[:, dc:dc + 1])
                    ins.then_inc(ev, 1)
                ctx.inc("ev", 1)
                ctx.emit("vector", fin)
            emit_wait(ctx, "sync", "ev", ctx.val("ev"))
            emit_dma(ctx, "sync", xo_d[dc * 128:(dc + 1) * 128, t0:t0 + 1024],
                     xin[:, :], "st")
    emit_wait(ctx, "sync", "st", ctx.val("st"))
    return ctx.val("ev")


def ffn1_inputs(inp):
    return {
        "f1_w1T": np.ascontiguousarray(inp["L1_ffn_W1"].T),
        "f1_w2T": np.ascontiguousarray(inp["L1_ffn_W2"].T),
        "f1_b1": np.ascontiguousarray(inp["L1_ffn_b1"].reshape(32, 128).T),
        "f1_b2": np.ascontiguousarray(inp["L1_ffn_b2"].reshape(8, 128).T),
        "f1_scale": inp["L1_ffn_scale"].reshape(1, 1),
    }





def emit_s7(ctx, x_d, xo_d, dep_ev, tag="s7"):
    nc = ctx.nc
    H, HD = 16, 64
    wqkv_d = ctx.dram("s7_wqkv", [D, 3 * D])
    wout_d = ctx.dram("s7_wout", [D, D])
    gam_d = ctx.dram("s7_gamma", [128, 8])
    bet_d = ctx.dram("s7_beta", [128, 8])

    y0 = ctx.sb("sc_y0", [128, FC * T], BF16)
    y2 = ctx.sb("sc_big_y", [128, FC * T], BF16)
    osb = ctx.sb("sc_osb", [128, FC * T], BF16)
    qp = ctx.sb("sc_qp", [128, T], BF16)
    kp = ctx.sb("sc_kp", [128, T], BF16)
    vp = ctx.sb("sc_vp", [128, T], BF16)
    tmp = ctx.sb("sc_tmp7", [128, T], F32)
    ktm = ctx.sb("sc_ktm", [128, 16 * 128], BF16)
    vtm = ctx.sb("sc_vtm", [128, 16 * 128], BF16)
    kvsb = ctx.sb("sc_kv", [128, 128], BF16)
    kssb = ctx.sb("sc_ks", [128, 2], BF16)
    denr = ctx.sb("sc_denr", [1, T], F32)
    binv = ctx.sb("sc_binv", [64, TBW], BF16)
    wsb = ctx.sb("sc_w", [128, FC * D], BF16)
    wkv = ctx.sb("sc_wkv7", [128, FC * 384], BF16)
    gcol = ctx.sb("sc_gam", [128, 8], F32)
    bcol = ctx.sb("sc_bet", [128, 8], F32)
    xin = ctx.sb("sc_xin", [128, T], F32)

    pln0 = ctx.ps("pln0", [1, TBW])
    p512a = ctx.ps("p512a", [128, TBW])
    p512b = ctx.ps("p512b", [128, TBW])
    p512c = ctx.ps("p512c", [128, TBW])
    prow = ctx.ps("prow", [1, TBW])
    pkv = ctx.ps("pkv", [128, 128], BF16)
    ones = ctx.sbuf["c_ones"]
    onesrowf = ctx.sbuf["c_onesrowf"]
    idb = ctx.sbuf["c_idb"]
    mm = ctx.sem("mm"); ev = ctx.sem("ev"); ac = ctx.sem("ac")

    wlg = emit_dma(ctx, "gpsimd", gcol[:, :], gam_d[:, :], "wl")
    wlb = emit_dma(ctx, "gpsimd", bcol[:, :], bet_d[:, :], "wl")

    ln_ev = emit_layernorm(ctx, tag + "a", None, y0, dep=("ev", dep_ev), x_dram=x_d)
    emit_wait(ctx, "vector", "wl", wlb)
    ln2_ev = emit_layernorm(ctx, tag + "b", y0, y2, gamma_col=gcol, beta_col=bcol,
                            dep=("ev", ln_ev))

    for pr in range(8):
        emit_wait(ctx, "gpsimd", "mm", ctx.val("mm"))
        wl1 = emit_dma(ctx, "gpsimd",
                       wkv.ap().rearrange("p (fc g) -> p fc g", g=384)[:, :, 0:128],
                       wqkv_d.rearrange("(fc p) g -> p fc g", p=128)[:, :, pr * 128:pr * 128 + 128],
                       "wl")
        emit_dma(ctx, "gpsimd",
                 wkv.ap().rearrange("p (fc g) -> p fc g", g=384)[:, :, 128:256],
                 wqkv_d.rearrange("(fc p) g -> p fc g", p=128)[:, :, D + pr * 128:D + pr * 128 + 128],
                 "wl")
        wl3 = emit_dma(ctx, "gpsimd",
                       wkv.ap().rearrange("p (fc g) -> p fc g", g=384)[:, :, 256:384],
                       wqkv_d.rearrange("(fc p) g -> p fc g", p=128)[:, :, 2 * D + pr * 128:2 * D + pr * 128 + 128],
                       "wl")
        emit_wait(ctx, "tensor", "wl", wl3)
        emit_wait(ctx, "tensor", "ev", ctx.val("ev") if pr > 0 else ln2_ev)
        for which, dst in ((0, qp), (1, kp), (2, vp)):
            for tb in range(TB):
                pq = p512a if tb % 2 == 0 else p512b
                emit_wait(ctx, "tensor", "ev", ctx.val("ev"))
                def pm(t, which=which, tb=tb, pq=pq, wkv=wkv, y2=y2):
                    for fc in range(FC):
                        ins = t.matmul(pq[:, :],
                                       wkv[:, fc * 384 + which * 128:fc * 384 + which * 128 + 128],
                                       y2[:, fc * T + tb * TBW:fc * T + tb * TBW + TBW],
                                       start=(fc == 0), stop=(fc == FC - 1))
                    ins.then_inc(mm, 1)
                mp = ctx.inc("mm", 1)
                ctx.emit("tensor", pm)
                emit_wait(ctx, "vector", "mm", mp)
                if which < 2:
                    def pe(v, tb=tb, pq=pq, tmp=tmp, ev=ev, base=ctx.val("ev")):
                        ins = v.tensor_scalar_min(tmp[:, tb * TBW:tb * TBW + TBW],
                                                  pq[:, :], 0.0)
                        ins.then_inc(ev, 1)
                    v1 = ctx.inc("ev", 1)
                    ctx.emit("vector", pe)
                    emit_wait(ctx, "scalar", "ev", v1)
                    def px(s, tb=tb, tmp=tmp):
                        ins = s.activation(out=tmp[:, tb * TBW:tb * TBW + TBW],
                                           in_=tmp[:, tb * TBW:tb * TBW + TBW],
                                           func=AF.Exp)
                        ins.then_inc(ac, 1)
                    a1 = ctx.inc("ac", 1)
                    ctx.emit("scalar", px)
                    emit_wait(ctx, "vector", "ac", a1)
                    def pf(v, tb=tb, pq=pq, tmp=tmp, dst=dst):
                        ins = v.scalar_tensor_tensor(dst[:, tb * TBW:tb * TBW + TBW],
                                                     pq[:, :], 0.0,
                                                     tmp[:, tb * TBW:tb * TBW + TBW],
                                                     op0=ALU.max, op1=ALU.add)
                        ins.then_inc(ev, 1)
                    ctx.inc("ev", 1)
                    ctx.emit("vector", pf)
                else:
                    def pv(v, tb=tb, pq=pq, dst=dst):
                        ins = v.tensor_copy(dst[:, tb * TBW:tb * TBW + TBW], pq[:, :])
                        ins.then_inc(ev, 1)
                    ctx.inc("ev", 1)
                    ctx.emit("vector", pv)
        pr_ev = ctx.val("ev")
        emit_wait(ctx, "tensor", "ev", pr_ev)
        for srcb, dstb in ((kp, ktm), (vp, vtm)):
            for tc in range(16):
                emit_wait(ctx, "tensor", "ev", ctx.val("ev"))
                def ttr(t, srcb=srcb, tc=tc, pkv=pkv, idb=idb):
                    ins = t.transpose(pkv[:, :], srcb[:, tc * 128:(tc + 1) * 128],
                                      idb[:, :])
                    ins.then_inc(mm, 1)
                mt = ctx.inc("mm", 1)
                ctx.emit("tensor", ttr)
                emit_wait(ctx, "vector", "mm", mt)
                def tte(v, dstb=dstb, tc=tc, pkv=pkv):
                    ins = v.tensor_copy(dstb[:, tc * 128:(tc + 1) * 128], pkv[:, :])
                    ins.then_inc(ev, 1)
                ctx.inc("ev", 1)
                ctx.emit("vector", tte)
        tm_ev = ctx.val("ev")
        emit_wait(ctx, "tensor", "ev", tm_ev)
        for hh in range(2):
            h = pr * 2 + hh
            hb = hh * 64
            gfc = pr
            def kvm(t, hh=hh, hb=hb, p512b=p512b, ktm=ktm, vtm=vtm):
                for tc in range(16):
                    ins = t.matmul(p512b[0:64, 0:64],
                                   ktm[:, tc * 128 + hb:tc * 128 + hb + 64],
                                   vtm[:, tc * 128 + hb:tc * 128 + hb + 64],
                                   start=(tc == 0), stop=(tc == 15))
                ins.then_inc(mm, 1)
            mkv = ctx.inc("mm", 1)
            ctx.emit("tensor", kvm)
            def ksm(t, hb=hb, p512c=p512c, ones=ones, ktm=ktm):
                for tc in range(16):
                    ins = t.matmul(p512c[0:64, 0:1],
                                   ktm[:, tc * 128 + hb:tc * 128 + hb + 64],
                                   ones[:, 0:1],
                                   start=(tc == 0), stop=(tc == 15))
                ins.then_inc(mm, 1)
            mks = ctx.inc("mm", 1)
            ctx.emit("tensor", ksm)
            emit_wait(ctx, "vector", "mm", mks)
            def kve(v, p512b=p512b, kvsb=kvsb, ev=ev, base=ctx.val("ev")):
                v.tensor_copy(kvsb[0:64, 0:64], p512b[0:64, 0:64]).then_inc(ev, 1)
                v.wait_ge(ev, base + 1)
                ins = v.tensor_copy(kvsb[64:128, 0:64], kvsb[0:64, 0:64])
                ins.then_inc(ev, 1)
            ctx.inc("ev", 1)
            vkv = ctx.inc("ev", 1)
            ctx.emit("vector", kve)
            emit_wait(ctx, "vector", "mm", mks)
            def kse(v, p512c=p512c, kssb=kssb):
                v.tensor_copy(kssb[0:64, 0:1], p512c[0:64, 0:1])
                ins = v.tensor_copy(kssb[64:128, 0:1], kssb[0:64, 0:1])
                ins.then_inc(ev, 1)
            vks = ctx.inc("ev", 1)
            ctx.emit("vector", kse)
            emit_wait(ctx, "tensor", "ev", vks)
            for tb in range(TB):
                emit_wait(ctx, "tensor", "ev", ctx.val("ev"))
                def dnm(t, hb=hb, tb=tb, prow=prow, kssb=kssb, qp=qp):
                    ins = t.matmul(prow[0:1, :], kssb[hb:hb + 64, 0:1],
                                   qp[hb:hb + 64, tb * TBW:tb * TBW + TBW],
                                   start=True, stop=True)
                    ins.then_inc(mm, 1)
                md = ctx.inc("mm", 1)
                ctx.emit("tensor", dnm)
                emit_wait(ctx, "vector", "mm", md)
                def dne(v, tb=tb, prow=prow, denr=denr, ev=ev, base=ctx.val("ev")):
                    v.tensor_scalar_add(denr[0:1, tb * TBW:tb * TBW + TBW],
                                        prow[0:1, :], 1e-6).then_inc(ev, 1)
                    v.wait_ge(ev, base + 1)
                    ins = v.reciprocal(denr[0:1, tb * TBW:tb * TBW + TBW],
                                       denr[0:1, tb * TBW:tb * TBW + TBW])
                    ins.then_inc(ev, 1)
                ctx.inc("ev", 2)
                ctx.emit("vector", dne)
            dn_ev = ctx.val("ev")
            emit_wait(ctx, "tensor", "ev", dn_ev)
            for tb in range(TB):
                emit_wait(ctx, "tensor", "ev", ctx.val("ev"))
                def om(t, hb=hb, tb=tb, p512a=p512a, kvsb=kvsb, qp=qp):
                    ins = t.matmul(p512a[0:64, :], kvsb[hb:hb + 64, 0:64],
                                   qp[hb:hb + 64, tb * TBW:tb * TBW + TBW],
                                   start=True, stop=True)
                    ins.then_inc(mm, 1)
                mo = ctx.inc("mm", 1)
                ctx.emit("tensor", om)
                def dbc(t, tb=tb, p512c=p512c, denr=denr, onesrowf=onesrowf):
                    ins = t.matmul(p512c[0:64, :], onesrowf[0:1, 0:64],
                                   denr[0:1, tb * TBW:tb * TBW + TBW],
                                   start=True, stop=True)
                    ins.then_inc(mm, 1)
                mb = ctx.inc("mm", 1)
                ctx.emit("tensor", dbc)
                emit_wait(ctx, "vector", "mm", mb)
                def odv(v, hh=hh, hb=hb, gfc=gfc, tb=tb, p512a=p512a, p512c=p512c,
                        binv=binv, osb=osb, ev=ev, base=ctx.val("ev")):
                    v.tensor_copy(binv[:, :], p512c[0:64, :]).then_inc(ev, 1)
                    v.wait_ge(ev, base + 1)
                    ins = v.tensor_tensor(osb[hb:hb + 64, gfc * T + tb * TBW:gfc * T + tb * TBW + TBW],
                                          p512a[0:64, :], binv[:, :], ALU.mult)
                    ins.then_inc(ev, 1)
                ctx.inc("ev", 2)
                ctx.emit("vector", odv)
            emit_wait(ctx, "tensor", "ev", ctx.val("ev"))
    att_ev = ctx.val("ev")
    emit_wait(ctx, "gpsimd", "mm", ctx.val("mm"))
    wlo = emit_dma(ctx, "gpsimd",
                   wsb.ap().rearrange("p (fc g) -> p fc g", g=D),
                   wout_d.rearrange("(fc p) g -> p fc g", p=128), "wl")
    emit_wait(ctx, "tensor", "ev", att_ev)
    emit_wait(ctx, "tensor", "wl", wlo)
    for dc in range(FC):
        if dc > 0:
            emit_wait(ctx, "gpsimd", "st", ctx.val("st"))
        lx = emit_dma(ctx, "gpsimd", xin[:, :], x_d[dc * 128:(dc + 1) * 128, :], "ld")
        for tb in range(TB):
            po = p512a if tb % 2 == 0 else p512b
            emit_wait(ctx, "tensor", "ev", ctx.val("ev"))
            def omm(t, dc=dc, tb=tb, po=po, wsb=wsb, osb=osb):
                for fc in range(FC):
                    ins = t.matmul(po[:, :],
                                   wsb[:, fc * D + dc * 128:fc * D + dc * 128 + 128],
                                   osb[:, fc * T + tb * TBW:fc * T + tb * TBW + TBW],
                                   start=(fc == 0), stop=(fc == FC - 1))
                ins.then_inc(mm, 1)
            mo = ctx.inc("mm", 1)
            ctx.emit("tensor", omm)
            emit_wait(ctx, "vector", "mm", mo)
            emit_wait(ctx, "vector", "ld", lx)
            def oev(v, dc=dc, tb=tb, po=po, xin=xin, y0=y0):
                s = slice(tb * TBW, tb * TBW + TBW)
                sy = slice(dc * T + tb * TBW, dc * T + tb * TBW + TBW)
                v.tensor_tensor(xin[:, s], xin[:, s], po[:, :], ALU.add)
                ins = v.tensor_tensor(xin[:, s], xin[:, s], y0[:, sy], ALU.add)
                ins.then_inc(ev, 1)
            ctx.inc("ev", 1)
            ctx.emit("vector", oev)
        emit_wait(ctx, "sync", "ev", ctx.val("ev"))
        emit_dma(ctx, "sync", xo_d[dc * 128:(dc + 1) * 128, :], xin[:, :], "st")
    emit_wait(ctx, "sync", "st", ctx.val("st"))
    return ctx.val("ev")


def s5_inputs_core(inp, c):
    wih = inp["L1_attn_a_w_ih"]
    whh = inp["L1_attn_a_w_hh"]
    rows = np.r_[256 * c:256 * c + 256,
                 2048 + 256 * c:2048 + 256 * c + 256,
                 4096 + 256 * c:4096 + 256 * c + 256]
    wihT = np.ascontiguousarray(wih[rows].T)
    whhT = np.ascontiguousarray(whh[rows].T)
    perm = np.array([r * 256 + q * 128 + p
                     for q in range(2) for r in range(8) for p in range(128)])
    whhT = np.ascontiguousarray(whhT[perm])
    wot = inp["L1_attn_a_w_out"][:, 256 * c:256 * c + 256]
    return {
        "s5_gamma": np.ascontiguousarray(inp["L1_attn_a_gamma"].reshape(8, 128).T),
        "s5_beta": np.ascontiguousarray(inp["L1_attn_a_beta"].reshape(8, 128).T),
        "s5_wihT": wihT,
        "s5_whhT": whhT,
        "s5_bih": np.ascontiguousarray(inp["L1_attn_a_b_ih"][rows].reshape(1, 768)),
        "s5_bhh": np.ascontiguousarray(inp["L1_attn_a_b_hh"][rows].reshape(1, 768)),
        "s5_wotT": np.ascontiguousarray(wot.T),
        "s5_bout": np.ascontiguousarray(inp["L1_attn_a_b_out"].reshape(8, 128).T),
    }


def s7_inputs(inp):
    return {
        "s7_wqkv": inp["L1_attn_b_Wqkv"],
        "s7_wout": inp["L1_attn_b_Wout"],
        "s7_gamma": np.ascontiguousarray(inp["L1_attn_b_gamma"].reshape(8, 128).T),
        "s7_beta": np.ascontiguousarray(inp["L1_attn_b_beta"].reshape(8, 128).T),
    }





def emit_s5(ctx, x_d, xo_d, dep_ev):
    nc = ctx.nc
    HG = 768
    gam_d = ctx.dram("s5_gamma", [128, 8])
    bet_d = ctx.dram("s5_beta", [128, 8])
    wih_d = ctx.dram("s5_wihT", [D, HG])
    whh_d = ctx.dram("s5_whhT", [T, HG])
    bih_d = ctx.dram("s5_bih", [1, HG])
    bhh_d = ctx.dram("s5_bhh", [1, HG])
    wot_d = ctx.dram("s5_wotT", [256, D])
    bout_d = ctx.dram("s5_bout", [128, 8])

    NSTEP = 2048
    YBLK = 32
    HRING = 4

    y0 = ctx.sb("sc_y0", [128, FC * T], BF16)
    y = ctx.sb("sc_big_y", [128, FC * T], BF16)
    wih = ctx.sb("sc_wih5", [128, FC * HG], BF16)
    wt = ctx.sb("sc_whh5", [128, 16 * HG], BF16)
    wot = ctx.sb("sc_wot5", [128, 2 * D], BF16)
    gcol = ctx.sb("sc_gam", [128, 8], F32)
    bcol = ctx.sb("sc_bet", [128, 8], F32)
    bocol = ctx.sb("sc_bo5", [128, 8], F32)
    brow = ctx.sb("sc_brow5", [1, 4 * HG], F32)
    browb = ctx.sb("sc_browb5", [1, HG], BF16)
    bhn8 = ctx.sb("sc_bhn85", [8, 256], BF16)
    ybr = ctx.sb("sc_ybr5", [128, 2 * FC * 8 * YBLK], BF16)
    xis = ctx.sb("sc_xis5", [8, HG], BF16)
    rz = ctx.sb("sc_rz5", [8, 512], F32)
    nn = ctx.sb("sc_nn5", [8, 256], F32)
    aa_t = ctx.sb("sc_aa5", [8, 256], F32)
    cc_t = ctx.sb("sc_cc5", [8, 256], F32)
    u1 = ctx.sb("sc_u15", [8, 256], F32)
    h_bj = ctx.sb("sc_hbj5", [8, 256], F32)
    h_rb = ctx.sb("sc_hrb5", [64, 256], BF16)
    hb16 = ctx.sb("sc_hb165", [8, 256], BF16)
    hT = ctx.sb("sc_hT5", [128, 128], BF16)
    hTl = ctx.sb("sc_hTl5", [128, 16], BF16)
    yring = ctx.sb("sc_yring5", [8, HRING * D], BF16)
    ytr = ctx.sb("sc_ytr5", [128, T], BF16)
    xin = ctx.sb("sc_xin", [128, T], F32)

    bin_y = nc.dram_tensor("s5i_biny", [D, T], BF16)
    yall = nc.dram_tensor("s5i_yall", [8 * D, T], BF16)
    bin_h = nc.dram_tensor("s5i_binh", [8, 256], BF16)
    bout_h = nc.dram_tensor("s5i_bouth", [64, 256], BF16)
    ypart = nc.dram_tensor("s5i_ypart", [8 * T, D], BF16)
    yred = nc.dram_tensor("s5i_yred", [T, D], BF16)

    pln0 = ctx.ps("pln0", [1, TBW])
    pRZ = ctx.ps("p512a", [128, TBW])
    pN = ctx.ps("p512b", [128, TBW])
    po = ctx.ps("p512c", [128, TBW])
    ppt = ctx.ps("ppt5", [128, 128], BF16)
    pptl = ctx.ps("pptl5", [128, 16], BF16)
    ones = ctx.sbuf["c_ones"]
    onesrow = ctx.sbuf["c_onesrow"]
    idb = ctx.sbuf["c_idb"]
    mm = ctx.sem("mm"); ev = ctx.sem("ev"); ac = ctx.sem("ac"); gp = ctx.sem("gp")
    cc = ctx.sem("cc")
    sts = ctx.sem("st")

    poB = ctx.ps("poB5", [8, TBW])

    wl0 = emit_dma(ctx, "gpsimd", gcol[:, :], gam_d[:, :], "wl")
    emit_dma(ctx, "gpsimd", bcol[:, :], bet_d[:, :], "wl")
    emit_dma(ctx, "gpsimd", bocol[:, :], bout_d[:, :], "wl")
    emit_dma(ctx, "gpsimd",
             wih.ap().rearrange("p (fc g) -> p fc g", g=HG),
             wih_d.rearrange("(fc p) g -> p fc g", p=128), "wl")
    emit_dma(ctx, "gpsimd",
             wt.ap().rearrange("p (k g) -> p k g", g=HG),
             whh_d.rearrange("(k p) g -> p k g", p=128), "wl")
    emit_dma(ctx, "gpsimd",
             wot.ap().rearrange("p (q g) -> p q g", g=D),
             wot_d.rearrange("(q p) g -> p q g", p=128), "wl")
    emit_dma(ctx, "gpsimd", brow[0:1, 0:HG], bih_d[0:1, :], "wl")
    wlb = emit_dma(ctx, "gpsimd", brow[0:1, HG:2 * HG], bhh_d[0:1, :], "wl")

    emit_wait(ctx, "vector", "wl", wlb)
    def mkbias(v, brow=brow, browb=browb, ev=ev, base=ctx.val("ev")):
        v.tensor_copy(brow[0:1, 2 * HG:2 * HG + 512], brow[0:1, HG:HG + 512]).then_inc(ev, 1)
        v.wait_ge(ev, base + 1)
        v.memset(brow[0:1, 2 * HG + 512:3 * HG], 0.0).then_inc(ev, 1)
        v.wait_ge(ev, base + 2)
        v.tensor_tensor(brow[0:1, 2 * HG:3 * HG], brow[0:1, 2 * HG:3 * HG],
                        brow[0:1, 0:HG], ALU.add).then_inc(ev, 1)
        v.wait_ge(ev, base + 3)
        ins = v.tensor_copy(browb[0:1, :], brow[0:1, 2 * HG:3 * HG])
        ins.then_inc(ev, 1)
    ctx.inc("ev", 3)
    bias_ev = ctx.inc("ev", 1)
    ctx.emit("vector", mkbias)
    emit_wait(ctx, "tensor", "ev", bias_ev)
    onesrowf = ctx.sbuf["c_onesrowf"]
    def mkbhn(t, brow=brow, poB=poB, onesrowf=onesrowf):
        ins = t.matmul(poB[0:8, 0:256], onesrowf[0:1, 0:8],
                       brow[0:1, HG + 512:2 * HG],
                       start=True, stop=True)
        ins.then_inc(mm, 1)
    mbh = ctx.inc("mm", 1)
    ctx.emit("tensor", mkbhn)
    emit_wait(ctx, "vector", "mm", mbh)
    def bhne(v, poB=poB, bhn8=bhn8):
        ins = v.tensor_copy(bhn8[0:8, :], poB[0:8, 0:256])
        ins.then_inc(ev, 1)
    bhn_ev = ctx.inc("ev", 1)
    ctx.emit("vector", bhne)

    ln_ev = emit_layernorm(ctx, "s5a", None, y0, dep=("ev", dep_ev), x_dram=x_d)
    emit_wait(ctx, "vector", "wl", wl0)
    ln2_ev = emit_layernorm(ctx, "s5b", y0, y, gamma_col=gcol, beta_col=bcol,
                            dep=("ev", ln_ev))

    emit_wait(ctx, "gpsimd", "ev", ln2_ev)
    yb = emit_dma(ctx, "gpsimd",
                  bin_y.ap().rearrange("(fc p) t -> p fc t", p=128),
                  y.ap().rearrange("p (fc t) -> p fc t", t=T), "st")
    def agy(g, ccs=cc):
        g.collective_compute(
            "AllGather", ALU.bypass, replica_groups=[list(range(8))],
            ins=[bin_y.ap().opt()], outs=[yall.ap().opt()],
        ).then_inc(ccs)
    ctx.emit("gpsimd", lambda g, yb=yb: g.wait_ge(sts, yb))
    agy_cc = ctx.inc("cc", 1)
    ctx.emit("gpsimd", agy)

    def init0(g, h_bj=h_bj, hT=hT, hTl=hTl, gp=gp):
        g.memset(h_bj[:, :], 0.0)
        g.memset(hT[:, :], 0.0)
        g.memset(hTl[:, :], 0.0).then_inc(gp, 1)
    gpi = ctx.inc("gp", 1)
    ctx.emit("gpsimd", init0)
    emit_wait(ctx, "tensor", "gp", gpi)
    emit_wait(ctx, "tensor", "ev", max(bias_ev, bhn_ev))
    emit_wait(ctx, "tensor", "wl", ctx.val("wl"))

    for t_ in range(NSTEP):
        blk = t_ // YBLK
        ti = t_ % YBLK
        par = blk % 2
        if ti == 0:
            emit_wait(ctx, "gpsimd", "cc", agy_cc)
            if blk >= 2:
                emit_wait(ctx, "gpsimd", "mm", ctx.val("mm"))
            for fc in range(FC):
                lyb = emit_dma(
                    ctx, "gpsimd",
                    ybr.ap().rearrange("p (pp fc b t) -> p pp fc b t",
                                       pp=2, fc=FC, b=8)[:, par, fc],
                    yall.ap().rearrange("(b fc p) t -> p fc b t",
                                        fc=FC, p=128)[:, fc, :, blk * YBLK:(blk + 1) * YBLK],
                    "ld")
            ybl_ld = ctx.val("ld")
        emit_wait(ctx, "tensor", "ld", ybl_ld)
        emit_wait(ctx, "tensor", "ev", ctx.val("ev"))
        def xim(t, par=par, ti=ti, ybr=ybr, wih=wih, browb=browb, po=po, poB=poB,
                onesrow=onesrow):
            for fc in range(FC):
                off = par * (FC * 8 * YBLK) + fc * (8 * YBLK) + ti
                lhs = bass.AP(ybr, off, [[2 * FC * 8 * YBLK, 128], [YBLK, 8]])
                t.matmul(po[0:8, 0:512], lhs, wih[:, fc * HG:fc * HG + 512],
                         start=(fc == 0), stop=False)
                t.matmul(poB[0:8, 0:256], lhs, wih[:, fc * HG + 512:(fc + 1) * HG],
                         start=(fc == 0), stop=False)
            t.matmul(po[0:8, 0:512], onesrow[0:1, 0:8], browb[0:1, 0:512],
                     start=False, stop=True)
            ins = t.matmul(poB[0:8, 0:256], onesrow[0:1, 0:8], browb[0:1, 512:HG],
                           start=False, stop=True)
            ins.then_inc(mm, 1)
        mxi = ctx.inc("mm", 1)
        ctx.emit("tensor", xim)
        emit_wait(ctx, "vector", "mm", mxi)
        def xie(v, po=po, poB=poB, xis=xis):
            v.tensor_copy(xis[0:8, 0:512], po[0:8, 0:512])
            ins = v.tensor_copy(xis[0:8, 512:HG], poB[0:8, 0:256])
            ins.then_inc(ev, 1)
        xi_ev = ctx.inc("ev", 1)
        ctx.emit("vector", xie)
        emit_wait(ctx, "tensor", "ev", xi_ev)
        def gmm(t, hT=hT, wt=wt, xis=xis, bhn8=bhn8, idb=idb, pRZ=pRZ, pN=pN):
            for k in range(16):
                t.matmul(pRZ[0:8, :], hT[:, k * 8:(k + 1) * 8],
                         wt[:, k * HG:k * HG + 512],
                         start=(k == 0), stop=False)
                t.matmul(pN[0:8, 0:256], hT[:, k * 8:(k + 1) * 8],
                         wt[:, k * HG + 512:(k + 1) * HG],
                         start=(k == 0), stop=False)
            t.matmul(pRZ[0:8, :], idb[0:8, 0:8], xis[0:8, 0:512],
                     start=False, stop=True)
            ins = t.matmul(pN[0:8, 0:256], idb[0:8, 0:8], bhn8[0:8, :],
                           start=False, stop=True)
            ins.then_inc(mm, 1)
        mg = ctx.inc("mm", 1)
        ctx.emit("tensor", gmm)
        emit_wait(ctx, "scalar", "mm", mg)
        def sig(s, pRZ=pRZ, rz=rz):
            ins = s.activation(out=rz[0:8, :], in_=pRZ[0:8, :], func=AF.Sigmoid)
            ins.then_inc(ac, 1)
        asg = ctx.inc("ac", 1)
        ctx.emit("scalar", sig)
        emit_wait(ctx, "gpsimd", "ac", asg)
        def zax(g, rz=rz, h_bj=h_bj, aa_t=aa_t, cc_t=cc_t, gp=gp):
            g.tensor_tensor(aa_t[0:8, :], rz[0:8, 256:512], h_bj[0:8, :], ALU.mult)
            ins = g.tensor_scalar(cc_t[0:8, :], rz[0:8, 256:512], -1.0, 1.0,
                                  op0=ALU.mult, op1=ALU.add)
            ins.then_inc(gp, 1)
        gpz = ctx.inc("gp", 1)
        ctx.emit("gpsimd", zax)
        emit_wait(ctx, "vector", "ac", asg)
        def uu(v, pN=pN, rz=rz, u1=u1, xis=xis, ev=ev, base=ctx.val("ev")):
            v.tensor_tensor(u1[0:8, :], rz[0:8, 0:256], pN[0:8, 0:256],
                            ALU.mult).then_inc(ev, 1)
            v.wait_ge(ev, base + 1)
            ins = v.tensor_tensor(u1[0:8, :], u1[0:8, :], xis[0:8, 512:HG], ALU.add)
            ins.then_inc(ev, 1)
        ctx.inc("ev", 1)
        vu = ctx.inc("ev", 1)
        ctx.emit("vector", uu)
        emit_wait(ctx, "scalar", "ev", vu)
        def tnh(s, u1=u1, nn=nn):
            ins = s.activation(out=nn[0:8, :], in_=u1[0:8, :], func=AF.Tanh)
            ins.then_inc(ac, 1)
        atn = ctx.inc("ac", 1)
        ctx.emit("scalar", tnh)
        emit_wait(ctx, "vector", "ac", atn)
        emit_wait(ctx, "vector", "gp", gpz)
        def hnew(v, nn=nn, cc_t=cc_t, aa_t=aa_t, h_bj=h_bj, ev=ev, base=ctx.val("ev")):
            v.tensor_tensor(h_bj[0:8, :], cc_t[0:8, :], nn[0:8, :],
                            ALU.mult).then_inc(ev, 1)
            v.wait_ge(ev, base + 1)
            ins = v.tensor_tensor(h_bj[0:8, :], h_bj[0:8, :], aa_t[0:8, :], ALU.add)
            ins.then_inc(ev, 1)
        ctx.inc("ev", 1)
        hn_ev = ctx.inc("ev", 1)
        ctx.emit("vector", hnew)
        emit_wait(ctx, "gpsimd", "ev", hn_ev)
        if t_ > 0:
            emit_wait(ctx, "gpsimd", "cc", ctx.val("cc"))
        hout = emit_dma(ctx, "gpsimd", bin_h.ap(), h_bj[0:8, :], "hd")
        ctx.emit("gpsimd", lambda g, hout=hout: g.wait_ge(ctx.sems["hd"], hout))
        if t_ > 0:
            emit_wait(ctx, "gpsimd", "st", ctx.val("st"))
        def agh(g, ccs=cc):
            g.collective_compute(
                "AllGather", ALU.bypass, replica_groups=[list(range(8))],
                ins=[bin_h.ap().opt()], outs=[bout_h.ap().opt()],
            ).then_inc(ccs)
        agh_cc = ctx.inc("cc", 1)
        ctx.emit("gpsimd", agh)
        emit_wait(ctx, "sync", "cc", agh_cc)
        emit_wait(ctx, "sync", "mm", ctx.val("mm"))
        hin = emit_dma(ctx, "sync", h_rb[:, :], bout_h.ap(), "st")
        emit_wait(ctx, "gpsimd", "ev", hn_ev)
        emit_wait(ctx, "gpsimd", "mm", ctx.val("mm"))
        def hb16c(g, h_bj=h_bj, hb16=hb16, gp=gp):
            ins = g.tensor_copy(hb16[0:8, :], h_bj[0:8, :])
            ins.then_inc(gp, 1)
        gph = ctx.inc("gp", 1)
        ctx.emit("gpsimd", hb16c)
        emit_wait(ctx, "tensor", "st", hin)
        emit_wait(ctx, "tensor", "ev", ctx.val("ev"))
        emit_wait(ctx, "tensor", "gp", gph)
        def trs(t, h_rb=h_rb, hb16=hb16, ppt=ppt, pptl=pptl, idb=idb):
            t.transpose(ppt[:, 0:64], h_rb[:, 0:128], idb[0:64, 0:64])
            t.transpose(ppt[:, 64:128], h_rb[:, 128:256], idb[0:64, 0:64])
            t.transpose(pptl[:, 0:8], hb16[0:8, 0:128], idb[0:8, 0:8])
            ins = t.transpose(pptl[:, 8:16], hb16[0:8, 128:256], idb[0:8, 0:8])
            ins.then_inc(mm, 1)
        mtr = ctx.inc("mm", 1)
        ctx.emit("tensor", trs)
        emit_wait(ctx, "vector", "mm", mtr)
        def tre(v, ppt=ppt, pptl=pptl, hT=hT, hTl=hTl, ev=ev, base=ctx.val("ev")):
            v.tensor_copy(hT[:, :], ppt[:, :]).then_inc(ev, 1)
            v.wait_ge(ev, base + 1)
            ins = v.tensor_copy(hTl[:, :], pptl[:, :])
            ins.then_inc(ev, 1)
        ctx.inc("ev", 1)
        tr_ev = ctx.inc("ev", 1)
        ctx.emit("vector", tre)
        emit_wait(ctx, "tensor", "ev", tr_ev)
        def opm2(t, hTl=hTl, wot=wot, po=po, poB=poB):
            for q in range(2):
                t.matmul(po[0:8, :], hTl[:, q * 8:(q + 1) * 8],
                         wot[:, q * D:q * D + 512],
                         start=(q == 0), stop=(q == 1))
            for q in range(2):
                ins = t.matmul(poB[0:8, :], hTl[:, q * 8:(q + 1) * 8],
                               wot[:, q * D + 512:(q + 1) * D],
                               start=(q == 0), stop=(q == 1))
            ins.then_inc(mm, 1)
        mop = ctx.inc("mm", 1)
        ctx.emit("tensor", opm2)
        emit_wait(ctx, "vector", "mm", mop)
        ri = t_ % HRING
        def ope(v, ri=ri, po=po, poB=poB, yring=yring):
            v.tensor_copy(yring[0:8, ri * D:ri * D + 512], po[0:8, :])
            ins = v.tensor_copy(yring[0:8, ri * D + 512:(ri + 1) * D], poB[0:8, :])
            ins.then_inc(ev, 1)
        ctx.inc("ev", 1)
        ctx.emit("vector", ope)
        if ri == HRING - 1:
            yfl = ctx.val("ev")
            emit_wait(ctx, "gpsimd", "ev", yfl)
            t0r = t_ - HRING + 1
            emit_dma(ctx, "gpsimd",
                     ypart.ap().rearrange("(b t) d -> b t d", b=8)[:, t0r:t0r + HRING, :],
                     yring.ap().rearrange("p (i d) -> p i d", d=D), "yd")
    rec_ev = ctx.val("ev")

    ydl = ctx.val("yd")
    def rsy(g, ccs=cc, ydl=ydl):
        g.wait_ge(ctx.sems["yd"], ydl)
        g.collective_compute(
            "ReduceScatter", ALU.add, replica_groups=[list(range(8))],
            ins=[ypart.ap().opt()], outs=[yred.ap().opt()],
        ).then_inc(ccs)
    rs_cc = ctx.inc("cc", 1)
    ctx.emit("gpsimd", rsy)

    emit_wait(ctx, "sync", "cc", rs_cc)
    for dc in range(FC):
        if dc > 0:
            emit_wait(ctx, "gpsimd", "st", ctx.val("st"))
            emit_wait(ctx, "sync", "st", ctx.val("st"))
        lx = emit_dma(ctx, "gpsimd", xin[:, :], x_d[dc * 128:(dc + 1) * 128, :], "ld")
        ltr = emit_dma_transpose(ctx, "sync", ytr[:, :],
                                 yred.ap()[:, dc * 128:(dc + 1) * 128], "st")
        emit_wait(ctx, "vector", "ld", lx)
        emit_wait(ctx, "vector", "st", ltr)
        def fin(v, dc=dc, xin=xin, ytr=ytr, y0=y0, bocol=bocol, ev=ev,
                base=ctx.val("ev")):
            v.tensor_scalar(xin[:, :], xin[:, :], bocol[:, dc:dc + 1], None,
                            op0=ALU.add).then_inc(ev, 1)
            v.wait_ge(ev, base + 1)
            v.tensor_tensor(xin[:, :], xin[:, :], ytr[:, :], ALU.add).then_inc(ev, 1)
            v.wait_ge(ev, base + 2)
            ins = v.tensor_tensor(xin[:, :], xin[:, :],
                                  y0[:, dc * T:(dc + 1) * T], ALU.add)
            ins.then_inc(ev, 1)
        ctx.inc("ev", 2)
        vf = ctx.inc("ev", 1)
        ctx.emit("vector", fin)
        emit_wait(ctx, "sync", "ev", vf)
        emit_dma(ctx, "sync", xo_d[dc * 128:(dc + 1) * 128, :], xin[:, :], "st")
    emit_wait(ctx, "sync", "st", ctx.val("st"))
    return ctx.val("ev")



_PROG_CACHE = {}


def build_sub_program(kind):
    if kind == "s4":
        return build_sub_program("s2")
    if kind == "s8":
        return build_sub_program("s6")
    if kind in _PROG_CACHE:
        return _PROG_CACHE[kind]
    nc = bass.Bass("TRN2", target_bir_lowering=False)
    ctx = Ctx(nc)
    x_d = nc.dram_tensor("x", [D, T], F32, kind="ExternalInput").ap()
    xo_d = nc.dram_tensor("xo", [D, T], F32, kind="ExternalOutput").ap()
    load_const_tiles(ctx)
    if kind == "s1":
        xres = ctx.sb("xres", [128, FC * T], F32)
        emit_dma(ctx, "gpsimd",
                 xres.ap().rearrange("p (fc t) -> p fc t", t=T),
                 x_d.rearrange("(fc p) t -> p fc t", p=128), "ld")
    for e in ("vector", "scalar", "tensor"):
        emit_wait(ctx, e, "ld", ctx.val("ld"))
        emit_wait(ctx, e, "gp", 1)
    ev0 = ctx.inc("ev", 1)
    ev = ctx.sem("ev")
    def seed(v, ev=ev):
        v.engine_nop().then_inc(ev, 1)
    ctx.emit("vector", seed)

    if kind == "s1":
        cur = emit_s1(ctx, xres, ev0)
        emit_wait(ctx, "sync", "ev", cur)
        st = ctx.inc("st", 16)
        sts = ctx.sem("st")
        def out(s):
            s.dma_start(out=xo_d.rearrange("(fc p) t -> p fc t", p=128),
                        in_=ctx.sbuf["xres"].ap().rearrange("p (fc t) -> p fc t", t=T)
                        ).then_inc(sts, 16)
            s.wait_ge(sts, st)
        ctx.emit("sync", out)
    elif kind in ("s2", "s4"):
        emit_moe(ctx, x_d, xo_d, ev0, kind)
    elif kind == "s3":
        emit_s3(ctx, x_d, xo_d, ev0)
    elif kind in ("s6", "s8"):
        emit_ffn1(ctx, x_d, xo_d, ev0, kind)
    elif kind == "s7":
        emit_s7(ctx, x_d, xo_d, ev0)
    elif kind == "s5":
        emit_s5(ctx, x_d, xo_d, ev0)
    nc = ctx.build()
    _PROG_CACHE[kind] = nc
    return nc


SUB_INPUTS = {
    "s1": lambda inp: s1_inputs(inp),
    "s2": lambda inp: moe_inputs(inp),
    "s3": lambda inp: s3_inputs(inp),
    "s4": lambda inp: moe_inputs(inp),
    "s6": lambda inp: ffn1_inputs(inp),
    "s5": None,
    "s7": lambda inp: s7_inputs(inp),
    "s8": lambda inp: ffn1_inputs(inp),
}


def run_chain(kinds, inputs):
    x = inputs["L0_attn_a_x"]
    xs = [np.ascontiguousarray(x[c].T) for c in range(NCORE)]
    for kind in kinds:
        nc = build_sub_program(kind)
        if kind == "s5":
            in_maps = [dict(s5_inputs_core(inputs, c), x=xs[c],
                            consts=host_consts()) for c in range(NCORE)]
        else:
            extra = SUB_INPUTS[kind](inputs)
            in_maps = [dict(extra, x=xs[c], consts=host_consts())
                       for c in range(NCORE)]
        res = run_bass_kernel_spmd(nc, in_maps, core_ids=list(range(NCORE)))
        xs = [np.ascontiguousarray(r["xo"]) for r in res.results]
    return np.stack([np.ascontiguousarray(xc.T) for xc in xs], 0)


ORDER = ["s1", "s2", "s3", "s4", "s5", "s6", "s7", "s8"]


def kernel_prefix(n_sub, inputs):
    return run_chain(ORDER[:n_sub], inputs)


def moe_inputs(inp):
    b1 = inp["L0_ffn_b1"]
    b2 = inp["L0_ffn_b2"]
    return {
        "moe_wred": inp["L0_ffn_w_red"],
        "moe_wg": inp["L0_ffn_wg"],
        "moe_w1": inp["L0_ffn_W1"],
        "moe_b1": np.ascontiguousarray(
            b1.reshape(8, 16, 128).transpose(2, 0, 1).reshape(128, 128)),
        "moe_w2": inp["L0_ffn_W2"],
        "moe_b2": np.ascontiguousarray(b2.reshape(1, 8 * 1024)),
    }


def kernel(**inputs):
    return run_chain(ORDER, inputs)
